# version 14
# baseline (speedup 1.0000x reference)
from contextlib import ExitStack

import numpy as np
import ml_dtypes

import concourse.bass as bass
import concourse.mybir as mybir
from concourse.bass_utils import run_bass_kernel_spmd

F32 = mybir.dt.float32
BF16 = mybir.dt.bfloat16
AX = mybir.AxisListType.X
AF = mybir.ActivationFunctionType
OP = mybir.AluOpType
BFD = ml_dtypes.bfloat16

S = 2048
CH = 1024
NT = 8
NB = 2
CK = 512
NC4 = 4
HID = 2730
NJ = 22
DH = 64
WIN = 512
NEG = np.float32(-3.0e38)

STREAMS = ("te", "dve", "act", "syncq", "poolq")
NQC = 4
ACCOUNTS = ("te", "dve", "act", "plc") + tuple(f"sq{i}" for i in range(NQC)) \
    + tuple(f"pq{i}" for i in range(NQC))
QCLS = {"syncq": "sq", "poolq": "pq"}

F32_COLS = 27360
BF_COLS = 51680


class Plan:

    def __init__(self):
        self.ops = {e: [] for e in STREAMS}
        self.count = {a: 0 for a in ACCOUNTS}
        self.qidx = {q: 0 for q in QCLS}
        self.last_write = {}
        self.readers = {}
        self.watermark = {a: {} for a in ACCOUNTS}
        self.epoch_marks = {a: 0 for a in ACCOUNTS}

    def waits_for(self, stream, account, reads, writes):
        waits = {}

        def need(src, cnt):
            if cnt <= 0:
                return
            if src == account and src not in ("te", "dve", "act"):
                return
            if cnt > self.watermark[account].get(src, 0) and \
                    cnt > waits.get(src, 0):
                waits[src] = cnt

        for k in reads:
            w = self.last_write.get(k)
            if w:
                need(*w)
        for k in writes:
            w = self.last_write.get(k)
            if w:
                need(*w)
            for src, cnt in self.readers.get(k, {}).items():
                need(src, cnt)
        if writes:
            for src, cnt in self.epoch_marks.items():
                need(src, cnt)
        return waits

    def emit(self, stream, account, fn, waits, inc):
        for src, cnt in waits.items():
            if cnt > self.watermark[account].get(src, 0):
                self.watermark[account][src] = cnt
        self.count[account] += inc
        self.ops[stream].append((waits, fn, inc, account))
        return self.count[account]

    def record(self, account, cnt, reads, writes):
        for k in reads:
            self.readers.setdefault(k, {})[account] = cnt
        for k in writes:
            self.last_write[k] = (account, cnt)
            self.readers[k] = {}

    def op(self, stream, fn, reads=(), writes=(), inc=1):
        if stream in QCLS:
            i = self.qidx[stream]
            self.qidx[stream] = i + 1
            account = f"{QCLS[stream]}{i % NQC}"
            waits = self.waits_for(stream, account, reads, writes)
            prior = 16 * (i // NQC)
            if prior > 0:
                waits[account] = prior
            cnt = self.emit(stream, account, fn, waits, 16)
            self.record(account, cnt, reads, writes)
        else:
            account = stream
            waits = self.waits_for(stream, account, reads, writes)
            cnt = self.emit(stream, account, fn, waits, inc)
            self.record(account, cnt, reads, writes)

    def epoch(self):
        self.epoch_marks = dict(self.count)

    def replay(self, name, eng, sems):
        for waits, fn, inc, account in self.ops[name]:
            for src in sorted(waits):
                eng.wait_ge(sems[src], waits[src])
            ins = fn(eng)
            if inc and ins is not None:
                ins.then_inc(sems[account], inc)


class Region:
    def __init__(self, tensor, off, cols, name):
        self.tensor = tensor
        self.off = off
        self.cols = cols
        self.key = name

    def ap(self, p0=0, p1=128, c0=0, c1=None):
        c1 = self.cols if c1 is None else c1
        assert 0 <= c0 <= c1 <= self.cols, (self.key, c0, c1, self.cols)
        return self.tensor[p0:p1, self.off + c0:self.off + c1]


class Alloc:
    def __init__(self, tensors, sizes):
        self.t = tensors
        self.size = sizes
        self.off = {k: 0 for k in tensors}
        self.marks = []
        self.peak = {k: 0 for k in tensors}
        self.on_release = None

    def buf(self, dtype, cols, name):
        o = (self.off[dtype] + 15) // 16 * 16
        self.off[dtype] = o + cols
        self.peak[dtype] = max(self.peak[dtype], self.off[dtype])
        assert self.off[dtype] <= self.size[dtype], (name, dtype, self.off[dtype])
        return Region(self.t[dtype], o, cols, name)

    def mark(self):
        self.marks.append(dict(self.off))

    def release(self):
        self.off = self.marks.pop()
        if self.on_release is not None:
            self.on_release()


def _mk_tables():
    half = np.arange(0, DH, 2, dtype=np.float64)
    inv_freq = 1.0 / 10000.0 ** (half / DH)
    t = np.arange(S, dtype=np.float64)
    freqs = t[:, None] * inv_freq[None, :]
    freqs = np.concatenate([freqs, freqs], axis=1)
    base = (half + 0.4 * DH) / (1.4 * DH)
    base = np.concatenate([base, base])
    power = (t - S // 2) / float(WIN // 2)
    scale = base[None, :] ** power[:, None]
    cos, sin = np.cos(freqs), np.sin(freqs)
    sign = np.ones(DH)
    sign[DH // 2:] = -1.0
    out = {}
    for nm, tab in (("csq", 8.0 * cos * scale), ("ssq", 8.0 * sin * scale * sign),
                    ("csk", cos / scale), ("ssk", sin / scale * sign)):
        tt = np.ascontiguousarray(tab.T)
        tt = np.concatenate([tt, tt], axis=0)
        hi = tt.astype(np.float32)
        lo = (tt - hi.astype(np.float64)).astype(np.float32)
        out[nm + "_hi"] = np.ascontiguousarray(hi)
        out[nm + "_lo"] = np.ascontiguousarray(lo)
    return out


def _mk_masks():
    m = np.zeros((2, 128, 128), np.float32)
    i = np.arange(128)[:, None]
    j = np.arange(128)[None, :]
    m[0] = np.where(j >= i, 0.0, NEG)
    m[1] = np.where(j <= i, 0.0, NEG)
    return m


def _mk_prot():
    p = np.zeros((128, 128), np.float32)
    for mcol in range(128):
        partner = mcol + 32 if (mcol % 64) < 32 else mcol - 32
        p[partner, mcol] = 1.0
    return p


def build_program():
    nc = bass.Bass("TRN2", target_bir_lowering=False)

    def din(name, shape, dt=F32):
        return nc.dram_tensor(name, shape, dt, kind="ExternalInput").ap()

    x_in = din("x", [NB, CH, S])
    Wg = [din(n, [CH, CH]) for n in ("Wf", "Wi", "Wv")]
    Wo = din("Wo", [CH, CH])
    wqkv = din("wqkv", [2, CH, 1536])
    wof = din("wof", [2, 512, CH])
    w1 = din("w1", [2, CH, 2 * HID])
    w2 = din("w2", [2, HID, CH])
    qsc = din("qsc", [2, DH])
    ksc = din("ksc", [2, DH])
    lng = din("lng", [2, CH])
    lnb = din("lnb", [2, CH])
    b1 = din("b1", [2, 2 * HID])
    b2 = din("b2", [2, CH])
    masks_in = din("masks", [2, 128, 128])
    tabs_in = {nm: din(nm, [128, S]) for nm in
               ("csq_hi", "csq_lo", "ssq_hi", "ssq_lo",
                "csk_hi", "csk_lo", "ssk_hi", "ssk_lo")}
    idf_in = din("idf", [128, 128])
    idb_in = din("idb", [128, 128], BF16)
    prot_in = din("prot", [128, 128], BF16)
    onec_in = din("onec", [128, 1], BF16)
    oner_in = din("oner", [1, 128])
    e2_in = din("e2", [128, 2], BF16)
    eb_in = din("eb", [2, 128])

    y_out = nc.dram_tensor("y", [NB, CH, S], F32, kind="ExternalOutput").ap()
    x1sp = nc.dram_tensor("x1sp", [NT, 128, S], F32, kind="Internal").ap()

    P = Plan()
    ctx = ExitStack()
    a32 = ctx.enter_context(nc.sbuf_tensor("a32", [128, F32_COLS], F32))
    a16 = ctx.enter_context(nc.sbuf_tensor("a16", [128, BF_COLS], BF16))
    AL = Alloc({F32: a32, BF16: a16}, {F32: F32_COLS, BF16: BF_COLS})
    AL.on_release = lambda: P.epoch()
    psum = ctx.enter_context(nc.psum_tensor("ps", [128, 4096], F32))
    sems = {a: ctx.enter_context(nc.semaphore("sem_" + a)) for a in ACCOUNTS}

    def bank(i, p0=0, p1=128, c0=0, c1=512):
        assert 0 <= c0 <= c1 <= 512
        return psum[p0:p1, i * 512 + c0:i * 512 + c1]

    ident_f = AL.buf(F32, 128, "identf")
    oner = AL.buf(F32, 128, "oner")
    eb = AL.buf(F32, 128, "eb")
    rowM = AL.buf(F32, CK, "rowM")
    rowV = AL.buf(F32, CK, "rowV")
    rowI = AL.buf(F32, CK, "rowI")
    rowR = AL.buf(F32, CK, "rowR")
    rowT = AL.buf(F32, S, "rowT")
    rql = AL.buf(F32, 32, "rql")
    x_cm = [AL.buf(F32, S + 1, f"x{t}") for t in range(NT)]
    ident_b = AL.buf(BF16, 128, "identb")
    prot = AL.buf(BF16, 128, "prot")
    onec = AL.buf(BF16, 16, "onec")
    e2 = AL.buf(BF16, 16, "e2")
    prm = AL.buf(F32, 144, "prm")
    def _sub(l, off, cols):
        return Region(prm.tensor, prm.off + l * 72 + off, cols, "prm")
    g_pp = [_sub(l, 0, 8) for l in range(2)]
    bb_pp = [_sub(l, 8, 8) for l in range(2)]
    qs_pp = [_sub(l, 16, 1) for l in range(2)]
    ks_pp = [_sub(l, 17, 1) for l in range(2)]
    b1a_pp = [_sub(l, 18, NJ) for l in range(2)]
    b1g_pp = [_sub(l, 40, NJ) for l in range(2)]
    b2_pp = [_sub(l, 62, 8) for l in range(2)]
    CST = [None]
    TABS = [None]
    MASKS = [None]

    def dma(q, out_ap, in_ap, reads=(), writes=(), slow=False):
        P.op(q, (lambda eng, o=out_ap, i=in_ap, s=slow: eng.dma_start(
            out=o, in_=i, allow_slow_non_contiguous=s)),
             reads=reads, writes=writes)

    class Group:

        def __init__(self, bank_key, inc_each=False):
            self.bank_key = bank_key
            self.inc_each = inc_each
            self.first = True

        def member(self, o, lt, r, rd, last=False):
            w = (self.bank_key,) if self.first else ()
            waits = P.waits_for("te", "te", rd, w)
            st = self.first
            self.first = False
            sp = last
            inc = 1 if (self.inc_each or last) else 0
            cnt = P.emit("te", "te", (lambda eng, o=o, lt=lt, r=r, st=st, sp=sp:
                                      (eng.matmul(o, lt, r, start=st, stop=sp),
                                       None)[1] if not (sp or True) else
                                      eng.matmul(o, lt, r, start=st, stop=sp)),
                         waits, inc)
            P.record("te", cnt if inc else P.count["te"] + 1, rd,
                     (self.bank_key,) if last else ())

    def mm_members(members, bank_key):
        g = Group(bank_key)
        for i, (o, lt, r, rd) in enumerate(members):
            g.member(o, lt, r, rd, last=(i == len(members) - 1))

    def mmg(items, reads, writes):
        mm_members([(o, lt, r, reads if i == 0 else ())
                    for i, (o, lt, r) in enumerate(items)], writes[0])

    def mm_kmajor(nk, lhsT_of_k, rhs_of_kc, banks, reads_of_k, evac,
                  pslice=(0, 128)):
        first = [True] * len(banks)
        p0, p1 = pslice
        for k in range(nk):
            for ci, bk in enumerate(banks):
                rd = reads_of_k(k) if (ci == 0 or k == nk - 1) else ()
                w = (f"ps{bk}",) if first[ci] else ()
                waits = P.waits_for("te", "te", rd, w)
                st = first[ci]
                first[ci] = False
                sp = (k == nk - 1)
                inc = 1 if sp else 0
                lt = lhsT_of_k(k)
                r = rhs_of_kc(k, ci)
                cnt = P.emit("te", "te",
                             (lambda eng, lt=lt, r=r, st=st, sp=sp, bk=bk:
                              eng.matmul(bank(bk, p0, p1), lt, r, start=st, stop=sp)),
                             waits, inc)
                P.record("te", cnt if inc else P.count["te"] + 1, rd,
                         (f"ps{bk}",) if sp else ())
        for ci, bk in enumerate(banks):
            if evac is not None:
                evac(ci, bk)

    def mm1(out_ap, lt_ap, r_ap, reads, bank_key):
        P.op("te", (lambda eng, o=out_ap, lt=lt_ap, r=r_ap:
                    eng.matmul(o, lt, r, start=True, stop=True)),
             reads=reads, writes=(bank_key,), inc=1)

    def vop(fn, reads=(), writes=(), eng="dve"):
        if eng == "poolc":
            waits = P.waits_for("poolq", "plc", reads, writes)
            cnt = P.emit("poolq", "plc", fn, waits, 1)
            P.record("plc", cnt, reads, writes)
        else:
            P.op(eng, fn, reads=reads, writes=writes, inc=1)

    dma("syncq", ident_f.ap(), idf_in[:, :], (), ("identf",))
    dma("syncq", oner.ap(0, 1, 0, 128), oner_in[:, :], (), ("oner",))
    dma("syncq", eb.ap(0, 2, 0, 128), eb_in[:, :], (), ("eb",))
    dma("syncq", eb.ap(32, 34, 0, 128), eb_in[:, :], (), ("eb",))
    dma("syncq", ident_b.ap(), idb_in[:, :], (), ("identb",))
    dma("syncq", prot.ap(), prot_in[:, :], (), ("prot",))
    dma("syncq", onec.ap(0, 128, 0, 1), onec_in[:, :], (), ("onec",))
    dma("syncq", e2.ap(0, 128, 0, 2), e2_in[:, :], (), ("e2",))
    for l in range(2):
        dma("syncq", g_pp[l].ap(), lng[l, :].rearrange("(t p) -> p t", p=128),
            (), (g_pp[l].key,), slow=True)
        dma("syncq", bb_pp[l].ap(), lnb[l, :].rearrange("(t p) -> p t", p=128),
            (), (bb_pp[l].key,), slow=True)
        dma("syncq", qs_pp[l].ap(0, 64), qsc[l:l+1, :].transpose([1, 0]),
            (), (qs_pp[l].key,), slow=True)
        dma("syncq", qs_pp[l].ap(64, 128), qsc[l:l+1, :].transpose([1, 0]),
            (), (qs_pp[l].key,), slow=True)
        dma("syncq", ks_pp[l].ap(0, 64), ksc[l:l+1, :].transpose([1, 0]),
            (), (ks_pp[l].key,), slow=True)
        dma("syncq", ks_pp[l].ap(64, 128), ksc[l:l+1, :].transpose([1, 0]),
            (), (ks_pp[l].key,), slow=True)
        dma("syncq", b1a_pp[l].ap(0, 128, 0, 21),
            b1[l, 0:2688].rearrange("(j p) -> p j", p=128), (), (b1a_pp[l].key,), slow=True)
        dma("syncq", b1a_pp[l].ap(0, 42, 21, 22),
            b1[l:l+1, 2688:2730].transpose([1, 0]), (), (b1a_pp[l].key,), slow=True)
        dma("syncq", b1g_pp[l].ap(0, 128, 0, 21),
            b1[l, HID:HID + 2688].rearrange("(j p) -> p j", p=128), (), (b1g_pp[l].key,), slow=True)
        dma("syncq", b1g_pp[l].ap(0, 42, 21, 22),
            b1[l:l+1, HID + 2688:2 * HID].transpose([1, 0]), (), (b1g_pp[l].key,), slow=True)
        dma("syncq", b2_pp[l].ap(), b2[l, :].rearrange("(t p) -> p t", p=128),
            (), (b2_pp[l].key,), slow=True)
    for t in range(NT):
        vop(lambda eng, t=t: eng.memset(x_cm[t].ap(0, 128, 0, 1), 0.0),
            writes=(x_cm[t].key,))
    P.epoch()

    bmi = [0]

    def nbm():
        b = bmi[0] % 4
        bmi[0] += 1
        return b

    def stats_chunk(c, sum_tiles, xsrc):
        if sum_tiles is not None:
            mems = [(bank(7, 0, 1, 0, 512), onec.ap(0, 128, 0, 1),
                     sum_tiles[t].ap(0, 128, c * CK, (c + 1) * CK),
                     (sum_tiles[t].key, "onec")) for t in range(NT)]
            mm_members(mems, "ps7")
        else:
            g = Group("ps7", inc_each=True)
            for t in range(NT):
                cb = CST[0][t % 2]
                vop(lambda eng, t=t, cb=cb: eng.copy(cb.ap(), xsrc(t)),
                    reads=(f"x{t}",), writes=(cb.key,), eng="act")
                g.member(bank(7, 0, 1, 0, 512), onec.ap(0, 128, 0, 1),
                         cb.ap(), (cb.key, "onec"), last=(t == NT - 1))
        vop(lambda eng: eng.tensor_scalar(rowM.ap(0, 1), bank(7, 0, 1, 0, 512),
                                          1.0 / CH, None, OP.mult),
            reads=("ps7",), writes=("rowM",))
        g = Group("ps7", inc_each=True)
        for t in range(NT):
            cb = CST[0][t % 2]
            vop(lambda eng, t=t, cb=cb: eng.activation(cb.ap(), xsrc(t), AF.Square),
                reads=(f"x{t}",), writes=(cb.key,), eng="act")
            g.member(bank(7, 32, 33, 0, 512), onec.ap(0, 128, 0, 1),
                     cb.ap(), (cb.key, "onec"), last=(t == NT - 1))
        vop(lambda eng: eng.activation(rowV.ap(0, 1), rowM.ap(0, 1), AF.Square),
            reads=("rowM",), writes=("rowV",), eng="act")
        vop(lambda eng: eng.tensor_scalar(rowV.ap(0, 1), rowV.ap(0, 1),
                                          1e-5, None, OP.subtract),
            reads=("rowV",), writes=("rowV",))
        vop(lambda eng: eng.scalar_tensor_tensor(
            rowV.ap(0, 1), bank(7, 32, 33, 0, 512), 1.0 / CH, rowV.ap(0, 1),
            OP.mult, OP.subtract),
            reads=("ps7", "rowV"), writes=("rowV",))
        vop(lambda eng: eng.activation(rowV.ap(0, 1), rowV.ap(0, 1), AF.Sqrt),
            reads=("rowV",), writes=("rowV",), eng="act")
        vop(lambda eng: eng.reciprocal(rowI.ap(0, 1), rowV.ap(0, 1)),
            reads=("rowV",), writes=("rowI",))
        vop(lambda eng: eng.tensor_tensor(rowR.ap(0, 1), rowM.ap(0, 1),
                                          rowI.ap(0, 1), OP.mult),
            reads=("rowM", "rowI"), writes=("rowR",))

    def bcast_rows():
        mm1(bank(5), oner.ap(0, 1, 0, 128), rowI.ap(0, 1, 0, 512),
            ("rowI", "oner"), "ps5")
        mm1(bank(6), oner.ap(0, 1, 0, 128), rowR.ap(0, 1, 0, 512),
            ("rowR", "oner"), "ps6")

    def outer_ln(sum_tiles):
        for c in range(NC4):
            stats_chunk(c, sum_tiles,
                        lambda t, c=c: x_cm[t].ap(0, 128, 1 + c * CK, 1 + (c + 1) * CK))
            bcast_rows()
            for t in range(NT):
                vop(lambda eng, t=t, c=c: eng.tensor_tensor(
                    x_cm[t].ap(0, 128, 1 + c * CK, 1 + (c + 1) * CK),
                    x_cm[t].ap(0, 128, 1 + c * CK, 1 + (c + 1) * CK),
                    bank(5), OP.mult),
                    reads=(f"x{t}", "ps5"), writes=(f"x{t}",))
                vop(lambda eng, t=t, c=c: eng.tensor_tensor(
                    x_cm[t].ap(0, 128, 1 + c * CK, 1 + (c + 1) * CK),
                    x_cm[t].ap(0, 128, 1 + c * CK, 1 + (c + 1) * CK),
                    bank(6), OP.subtract),
                    reads=(f"x{t}", "ps6"), writes=(f"x{t}",))

    def load_x(b):
        for t in range(NT):
            dma("syncq", x_cm[t].ap(0, 128, 1, S + 1),
                x_in[b, t * 128:(t + 1) * 128, :], (), (f"x{t}",))
        P.epoch()

    def ssm(b):
        AL.mark()
        x_bf = [AL.buf(BF16, S, f"xb{t}") for t in range(NT)]
        s_bf = [AL.buf(BF16, S, f"sb{t}") for t in range(NT)]
        fga = AL.buf(F32, S, "fga")
        fgb = AL.buf(F32, S, "fgb")
        fgc = AL.buf(F32, S, "fgc")
        wsl = [AL.buf(BF16, 3 * CH, f"wsl{i}") for i in range(2)]
        CST[0] = [AL.buf(BF16, CK, "cst0"), AL.buf(BF16, CK, "cst1")]
        for t in range(NT):
            vop(lambda eng, t=t: eng.copy(x_bf[t].ap(), x_cm[t].ap(0, 128, 1, S + 1)),
                reads=(f"x{t}",), writes=(x_bf[t].key,), eng="act")
        actf = (AF.Tanh, AF.Sigmoid, AF.Silu)
        outs = (fga, fgb, fgc)
        WgT = [Wg[gi].rearrange("(k p) n -> p k n", p=128) for gi in range(3)]
        WoT = Wo.rearrange("(k p) n -> p k n", p=128)
        for nt in range(NT):
            tr = wsl[nt % 2]
            for gi in range(3):
                dma("poolq", tr.ap(0, 128, gi * CH, (gi + 1) * CH).rearrange(
                    "p (k n) -> p k n", n=128),
                    WgT[gi][:, :, nt * 128:(nt + 1) * 128], (), (tr.key,))
            for gi in range(3):
                mm_kmajor(
                    8,
                    lambda k, gi=gi, tr=tr: tr.ap(0, 128, gi * CH + k * 128,
                                                  gi * CH + (k + 1) * 128),
                    lambda k, c: x_bf[k].ap(0, 128, c * CK, (c + 1) * CK),
                    [0, 1, 2, 3] if gi % 2 == 0 else [4, 5, 6, 7],
                    lambda k, tr=tr: (tr.key,) + tuple(x.key for x in x_bf),
                    lambda c, bk, gi=gi: vop(lambda eng, gi=gi, c=c, bk=bk: eng.activation(
                        outs[gi].ap(0, 128, c * CK, (c + 1) * CK), bank(bk), actf[gi]),
                        reads=(f"ps{bk}",), writes=(outs[gi].key,), eng="act"))
            vop(lambda eng: eng.tensor_tensor(fgb.ap(), fgb.ap(), fgc.ap(), OP.mult),
                reads=("fgb", "fgc"), writes=("fgb",))
            vop(lambda eng: eng.tensor_tensor_scan(fgc.ap(), fga.ap(), fgb.ap(),
                                                   0.0, OP.mult, OP.add),
                reads=("fga", "fgb"), writes=("fgc",))
            vop(lambda eng, nt=nt: eng.activation(s_bf[nt].ap(), fgc.ap(), AF.Silu),
                reads=("fgc",), writes=(s_bf[nt].key,), eng="act")
        for ht in range(NT):
            tr = wsl[ht % 2]
            dma("poolq", tr.ap(0, 128, 0, CH).rearrange("p (k n) -> p k n", n=128),
                WoT[:, :, ht * 128:(ht + 1) * 128], (), (tr.key,))
            mm_kmajor(
                8,
                lambda k, tr=tr: tr.ap(0, 128, k * 128, (k + 1) * 128),
                lambda k, c: s_bf[k].ap(0, 128, c * CK, (c + 1) * CK),
                [0, 1, 2, 3] if ht % 2 == 0 else [4, 5, 6, 7],
                lambda k, tr=tr: (tr.key,) + tuple(s.key for s in s_bf),
                lambda c, bk, ht=ht: vop(lambda eng, ht=ht, c=c, bk=bk: eng.tensor_tensor(
                    x_cm[ht].ap(0, 128, 1 + c * CK, 1 + (c + 1) * CK),
                    x_cm[ht].ap(0, 128, 1 + c * CK, 1 + (c + 1) * CK),
                    bank(bk), OP.add),
                    reads=(f"ps{bk}", f"x{ht}"), writes=(f"x{ht}",)))
        outer_ln(None)
        AL.release()
        P.epoch()

    def proc_qk(l, qr, sqb, rota, rotb, cs, ss, spp, dst, par):
        pr0 = 32 * par
        rkey = f"rowT{par}"
        sbk = 7 if par == 0 else 4
        for c in range(NC4):
            vop(lambda eng, c=c: eng.activation(
                sqb.ap(), qr.ap(0, 128, c * CK, (c + 1) * CK), AF.Square),
                reads=(qr.key,), writes=(sqb.key,), eng="act")
            mm1(bank(sbk, 0, 2, 0, 512), e2.ap(0, 128, 0, 2),
                sqb.ap(), (sqb.key, "e2"), f"ps{sbk}")
            vop(lambda eng, c=c: eng.tensor_scalar(
                rowT.ap(pr0, pr0 + 2, c * CK, (c + 1) * CK), bank(sbk, 0, 2, 0, 512),
                1e-24, None, OP.max),
                reads=(f"ps{sbk}",), writes=(rkey,))
        vop(lambda eng: eng.activation(rowT.ap(pr0, pr0 + 2), rowT.ap(pr0, pr0 + 2),
                                       AF.Sqrt),
            reads=(rkey,), writes=(rkey,), eng="act")
        vop(lambda eng: eng.reciprocal(rowT.ap(pr0, pr0 + 2), rowT.ap(pr0, pr0 + 2)),
            reads=(rkey,), writes=(rkey,))
        rbk = 6 if par == 0 else 5
        pbk = 5 if par == 0 else 6
        for c in range(NC4):
            mm1(bank(rbk), eb.ap(pr0, pr0 + 2, 0, 128),
                rowT.ap(pr0, pr0 + 2, c * CK, (c + 1) * CK),
                (rkey, "eb"), f"ps{rbk}")
            vop(lambda eng, c=c: eng.tensor_tensor(
                qr.ap(0, 128, c * CK, (c + 1) * CK),
                qr.ap(0, 128, c * CK, (c + 1) * CK), bank(rbk), OP.mult),
                reads=(qr.key, f"ps{rbk}"), writes=(qr.key,))
            vop(lambda eng, c=c: eng.scalar_tensor_tensor(
                rota.ap(), qr.ap(0, 128, c * CK, (c + 1) * CK), spp.ap(0, 128, 0, 1),
                cs.ap(0, 128, c * CK, (c + 1) * CK), OP.mult, OP.mult),
                reads=(qr.key, spp.key, cs.key), writes=(rota.key,))
            vop(lambda eng, c=c: eng.scalar_tensor_tensor(
                rotb.ap(), qr.ap(0, 128, c * CK, (c + 1) * CK), spp.ap(0, 128, 0, 1),
                ss.ap(0, 128, c * CK, (c + 1) * CK), OP.mult, OP.mult),
                reads=(qr.key, spp.key, ss.key), writes=(rotb.key,))
            mm1(bank(pbk), prot.ap(), rotb.ap(), (rotb.key, "prot"), f"ps{pbk}")
            vop(lambda eng, c=c: eng.tensor_tensor(
                dst.ap(0, 128, c * CK, (c + 1) * CK),
                rota.ap(), bank(pbk), OP.add),
                reads=(rota.key, f"ps{pbk}"), writes=(dst.key,))

    def attention(l, q_bf, k_bf, v_tm, o_bf, sim_f, exf, at_tp, att_t, smax, ssum):
        PAIRS = [(4, 5), (6, 7), (0, 1), (2, 3)]
        units = []
        for w in range(4):
            for ri in range(4):
                qt = w * 4 + ri
                for h in range(8):
                    units.append((w, ri, qt, h))
        avg = {}

        def geo(i):
            w, ri, qt, h = units[i]
            n = 5 if w > 0 else ri + 1
            kb0 = (w - 1) * 4 + ri if w > 0 else 0
            return w, ri, qt, h, n, kb0

        def emit_sim(i):
            w, ri, qt, h, n, kb0 = geo(i)
            th, par = h // 2, h % 2
            p0 = 64 * par
            sA, sB = PAIRS[i % 4]
            q0 = qt * 128
            W = n * 128
            W1 = min(W, 512)
            lt = q_bf[th].ap(p0, p0 + 64, q0, q0 + 128)
            mm1(bank(sA, 0, 128, 0, W1), lt,
                k_bf[th].ap(p0, p0 + 64, kb0 * 128, kb0 * 128 + W1),
                (q_bf[th].key, k_bf[th].key), f"ps{sA}")
            if W > 512:
                mm1(bank(sB, 0, 128, 0, W - 512), lt,
                    k_bf[th].ap(p0, p0 + 64, kb0 * 128 + 512, kb0 * 128 + W),
                    (q_bf[th].key, k_bf[th].key), f"ps{sB}")

        def emit_soft(i):
            w, ri, qt, h, n, kb0 = geo(i)
            sA, sB = PAIRS[i % 4]
            W = n * 128
            sf = sim_f[i % 4]
            ex = exf[i % 4]
            atp = at_tp[i % 4]

            def pch(c):
                if c < 4:
                    return bank(sA, 0, 128, c * 128, (c + 1) * 128), f"ps{sA}"
                return bank(sB, 0, 128, (c - 4) * 128, (c - 3) * 128), f"ps{sB}"
            lastc = n - 1
            if w > 0:
                pap, pkey = pch(0)
                vop(lambda eng, pap=pap, sf=sf: eng.tensor_tensor(
                    sf.ap(0, 128, 0, 128), pap,
                    MASKS[0].ap(0, 128, 0, 128), OP.add),
                    reads=(pkey, "masks"), writes=(sf.key,))
            pap, pkey = pch(lastc)
            vop(lambda eng, pap=pap, sf=sf, lastc=lastc: eng.tensor_tensor(
                sf.ap(0, 128, lastc * 128, (lastc + 1) * 128), pap,
                MASKS[0].ap(0, 128, 128, 256), OP.add),
                reads=(pkey, "masks"), writes=(sf.key,))
            c0m = 1 if w > 0 else 0
            if lastc > c0m:
                nm_ = min(4, lastc) - c0m
                if nm_ > 0:
                    vop(lambda eng, c0m=c0m, nm_=nm_, sA=sA, sf=sf: eng.copy(
                        sf.ap(0, 128, c0m * 128, (c0m + nm_) * 128),
                        bank(sA, 0, 128, c0m * 128, (c0m + nm_) * 128)),
                        reads=(f"ps{sA}",), writes=(sf.key,), eng="act")
                if lastc > 4:
                    vop(lambda eng, sB=sB, sf=sf: eng.copy(
                        sf.ap(0, 128, 512, 640), bank(sB, 0, 128, 0, 128)),
                        reads=(f"ps{sB}",), writes=(sf.key,), eng="act")
            vop(lambda eng, W=W, h=h, sf=sf: eng.reduce_max(
                smax.ap(0, 128, h, h + 1), sf.ap(0, 128, 0, W), axis=AX,
                negate=True),
                reads=(sf.key,), writes=("smax",))
            vop(lambda eng, W=W, h=h, sf=sf, ex=ex: eng.activation(
                ex.ap(0, 128, 0, W), sf.ap(0, 128, 0, W), AF.Exp,
                bias=smax.ap(0, 128, h, h + 1),
                accum_out=ssum.ap(0, 128, h, h + 1)),
                reads=(sf.key, "smax"), writes=(ex.key, "ssum"), eng="act")
            if h % 2 == 1:
                vop(lambda eng, h=h: eng.reciprocal(
                    smax.ap(0, 128, 7 + h, 9 + h), ssum.ap(0, 128, h - 1, h + 1)),
                    reads=("ssum",), writes=("smaxr",))
                exp_ = exf[(i - 1) % 4]
                atpp = at_tp[(i - 1) % 4]
                vop(lambda eng, W=W, h=h, atpp=atpp, exp_=exp_: eng.activation(
                    atpp.ap(0, 128, 0, W), exp_.ap(0, 128, 0, W), AF.Copy,
                    scale=smax.ap(0, 128, 7 + h, 8 + h)),
                    reads=(exp_.key, "smaxr"), writes=(atpp.key,), eng="act")
                vop(lambda eng, W=W, h=h, atp=atp, ex=ex: eng.activation(
                    atp.ap(0, 128, 0, W), ex.ap(0, 128, 0, W), AF.Copy,
                    scale=smax.ap(0, 128, 8 + h, 9 + h)),
                    reads=(ex.key, "smaxr"), writes=(atp.key,), eng="act")

        def emit_tail(i):
            w, ri, qt, h, n, kb0 = geo(i)
            th, par = h // 2, h % 2
            p0 = 64 * par
            sA, sB = PAIRS[i % 4]
            q0 = qt * 128
            W = n * 128
            W1 = min(W, 512)
            atp = at_tp[i % 4]
            att = att_t[i % 4]
            for c in range(n):
                bk = sA if c < 4 else sB
                cc = (c % 4) * 128
                mm1(bank(bk, 0, 128, cc, cc + 128),
                    atp.ap(0, 128, c * 128, (c + 1) * 128), ident_b.ap(),
                    (atp.key, "identb"), f"ps{bk}")
            vop(lambda eng, W1=W1, att=att, sA=sA: eng.tensor_copy(
                att.ap(0, 128, 0, W1), bank(sA, 0, 128, 0, W1)),
                reads=(f"ps{sA}",), writes=(att.key,))
            if n > 4:
                vop(lambda eng, att=att, sB=sB: eng.copy(
                    att.ap(0, 128, 512, 640), bank(sB, 0, 128, 0, 128)),
                    reads=(f"ps{sB}",), writes=(att.key,), eng="act")
            pe = (i - par) % 4
            avb = PAIRS[pe][1]
            g = Group(f"ps{avb}", inc_each=True)
            for c in range(n):
                g.member(bank(avb, p0, p0 + 64, 384, 512),
                         v_tm[th].ap(0, 128, (kb0 + c) * 128 + p0,
                                     (kb0 + c) * 128 + p0 + 64),
                         att.ap(0, 128, c * 128, (c + 1) * 128),
                         (v_tm[th].key, att.key),
                         last=(c == n - 1))
            if par == 1:
                vop(lambda eng, th=th, q0=q0, avb=avb: eng.tensor_copy(
                    o_bf[th].ap(0, 128, q0, q0 + 128),
                    bank(avb, 0, 128, 384, 512)),
                    reads=(f"ps{avb}",), writes=(o_bf[th].key,))

        NU = len(units)
        for i in range(NU + 2):
            if i < NU:
                emit_sim(i)
            if 1 <= i < NU + 1:
                emit_soft(i - 1)
            if i >= 2:
                emit_tail(i - 2)

    def ffn_layer(b, l):
        AL.mark()
        q_bf = [AL.buf(BF16, S, f"qbf{t}") for t in range(4)]
        k_bf = [AL.buf(BF16, S, f"kbf{t}") for t in range(4)]
        v_tm = [AL.buf(BF16, S, f"vtm{t}") for t in range(4)]

        AL.mark()
        TABS[0] = {nm: AL.buf(BF16, S, "tab" + nm)
                   for nm in ("csq", "ssq", "csk", "ssk")}
        tabs = TABS[0]
        AL.mark()
        thi = AL.buf(F32, S, "thi")
        tlo = AL.buf(F32, S, "tlo")
        for nm in ("csq", "ssq", "csk", "ssk"):
            dma("syncq", thi.ap(), tabs_in[nm + "_hi"][:, :], (), ("thi",))
            dma("syncq", tlo.ap(), tabs_in[nm + "_lo"][:, :], (), ("tlo",))
            vop(lambda eng, nm=nm: eng.tensor_tensor(tabs[nm].ap(), thi.ap(),
                                                     tlo.ap(), OP.add),
                reads=("thi", "tlo"), writes=(tabs[nm].key,))
        AL.release()
        h_bf = [AL.buf(BF16, S, f"hb{t}") for t in range(NT)]
        wqs = [AL.buf(BF16, CH, "wqs0")]
        qraw = [AL.buf(F32, S, "qraw0"), AL.buf(F32, S, "qraw1")]
        vcm = AL.buf(BF16, CK, "vcm")
        _sq = AL.buf(BF16, CK, "sqql")
        sqb = [_sq, _sq]
        rotb = sqb
        rota = [AL.buf(F32, CK, "rota0"), AL.buf(F32, CK, "rota1")]
        for t in range(NT):
            so = 1 if t < 4 else 0
            vop(lambda eng, t=t, so=so: eng.copy(h_bf[t].ap(),
                                                 x_cm[t].ap(0, 128, so, so + S)),
                reads=(f"x{t}",), writes=(h_bf[t].key,), eng="act")
        wq = wqkv[l].rearrange("(k p) n -> p k n", p=128)
        pending = [None]

        def flush_pending():
            if pending[0] is not None:
                m0 = pending[0]
                if m0 < 4:
                    proc_qk(l, qraw[m0 % 2], sqb[m0 % 2], rota[m0 % 2], rotb[m0 % 2],
                            tabs["csq"], tabs["ssq"], qs_pp[l], q_bf[m0], m0 % 2)
                else:
                    proc_qk(l, qraw[m0 % 2], sqb[m0 % 2], rota[m0 % 2], rotb[m0 % 2],
                            tabs["csk"], tabs["ssk"], ks_pp[l], k_bf[m0 - 4], m0 % 2)
                pending[0] = None

        for m in range(12):
            ws = wqs[0]
            dma("poolq", ws.ap().rearrange("p (k n) -> p k n", n=128),
                wq[:, :, m * 128:(m + 1) * 128], (), (ws.key,))

            def qkv_evac(c, bk, m=m):
                if m < 8:
                    qr = qraw[m % 2]
                    vop(lambda eng, c=c, bk=bk, qr=qr: eng.copy(
                        qr.ap(0, 128, c * CK, (c + 1) * CK), bank(bk)),
                        reads=(f"ps{bk}",), writes=(qr.key,), eng="act")
                else:
                    t = m - 8
                    vop(lambda eng, c=c, bk=bk: eng.copy(vcm.ap(), bank(bk)),
                        reads=(f"ps{bk}",), writes=("vcm",), eng="act")
                    for blk in range(4):
                        bk2 = 4 + (blk % 2)
                        mm1(bank(bk2, 0, 128, 0, 128),
                            vcm.ap(0, 128, blk * 128, (blk + 1) * 128),
                            ident_b.ap(), ("vcm", "identb"), f"ps{bk2}")
                        vop(lambda eng, t=t, c=c, blk=blk, bk2=bk2: eng.tensor_copy(
                            v_tm[t].ap(0, 128, (c * 4 + blk) * 128,
                                       (c * 4 + blk + 1) * 128),
                            bank(bk2, 0, 128, 0, 128)),
                            reads=(f"ps{bk2}",), writes=(v_tm[t].key,))
            mm_kmajor(
                8,
                lambda k, ws=ws: ws.ap(0, 128, k * 128, (k + 1) * 128),
                lambda k, c: h_bf[k].ap(0, 128, c * CK, (c + 1) * CK),
                [0, 1, 2, 3],
                lambda k, ws=ws: (ws.key,) + tuple(h.key for h in h_bf),
                qkv_evac)
            flush_pending()
            if m < 8:
                pending[0] = m
        flush_pending()
        AL.release()
        P.epoch()

        AL.mark()
        MASKS[0] = AL.buf(BF16, 2 * 128, "masks")
        for i in range(2):
            dma("poolq", MASKS[0].ap(0, 128, i * 128, (i + 1) * 128),
                masks_in[i, :, :], (), ("masks",))
        o_bf = [AL.buf(BF16, S, f"obf{t}") for t in range(4)]
        sim_f = [AL.buf(F32, 640, f"simf{i}") for i in range(4)]
        exf = [AL.buf(F32, 640, f"exf{i}") for i in range(4)]
        at_tp = [AL.buf(BF16, 640, f"attp{i}") for i in range(4)]
        att_t = [AL.buf(BF16, 640, f"att_t{i}") for i in range(4)]
        smax = AL.buf(F32, 16, "smax")
        ssum = AL.buf(F32, 16, "ssum")
        attention(l, q_bf, k_bf, v_tm, o_bf, sim_f, exf, at_tp, att_t, smax, ssum)

        wqs2 = [AL.buf(BF16, 512, f"wos{i}") for i in range(2)]
        wo_r = wof[l].rearrange("(k p) n -> p k n", p=128)
        for m in range(NT):
            ws = wqs2[m % 2]
            dma("poolq", ws.ap().rearrange("p (k n) -> p k n", n=128),
                wo_r[:, :, m * 128:(m + 1) * 128], (), (ws.key,))
            mm_kmajor(
                4,
                lambda k, ws=ws: ws.ap(0, 128, k * 128, (k + 1) * 128),
                lambda k, c: o_bf[k].ap(0, 128, c * CK, (c + 1) * CK),
                [0, 1, 2, 3],
                lambda k, ws=ws: (ws.key,) + tuple(o.key for o in o_bf),
                lambda c, bk, m=m: vop(lambda eng, m=m, c=c, bk=bk: eng.tensor_tensor(
                    x_cm[m].ap(0, 128, 1 + c * CK, 1 + (c + 1) * CK),
                    x_cm[m].ap(0, 128, 1 + c * CK, 1 + (c + 1) * CK),
                    bank(bk), OP.add),
                    reads=(f"ps{bk}", f"x{m}"), writes=(f"x{m}",)))
        AL.release()
        AL.release()
        P.epoch()

        AL.mark()
        hb = [AL.buf(BF16, S, f"h2b{t}") for t in range(NT)]
        CST[0] = [AL.buf(BF16, CK, "cst0"), AL.buf(BF16, CK, "cst1")]
        fga = AL.buf(F32, CK, "lnfga")
        for t in range(NT):
            so = 1 if t < 4 else 0
            vop(lambda eng, t=t, so=so: eng.copy(hb[t].ap(),
                                                 x_cm[t].ap(0, 128, so, so + S)),
                reads=(f"x{t}",), writes=(hb[t].key,), eng="act")
        for c in range(NC4):
            stats_chunk(c, hb,
                        lambda t, c=c: x_cm[t].ap(0, 128, (1 if t < 4 else 0) + c * CK,
                                                  (1 if t < 4 else 0) + (c + 1) * CK))
            bcast_rows()
            for t in range(NT):
                so = 1 if t < 4 else 0
                vop(lambda eng, t=t, c=c, so=so: eng.tensor_tensor(
                    fga.ap(), x_cm[t].ap(0, 128, so + c * CK, so + (c + 1) * CK),
                    bank(5), OP.mult),
                    reads=(f"x{t}", "ps5"), writes=("lnfga",))
                vop(lambda eng: eng.tensor_tensor(
                    fga.ap(), fga.ap(), bank(6), OP.subtract),
                    reads=("lnfga", "ps6"), writes=("lnfga",))
                vop(lambda eng, t=t, c=c, l=l: eng.tensor_scalar(
                    hb[t].ap(0, 128, c * CK, (c + 1) * CK), fga.ap(),
                    g_pp[l].ap(0, 128, t, t + 1), bb_pp[l].ap(0, 128, t, t + 1),
                    OP.mult, OP.add),
                    reads=("lnfga", g_pp[l].key, bb_pp[l].key), writes=(hb[t].key,))
        h3 = [AL.buf(BF16, 1024, f"h3_{j}") for j in range(NJ)]
        w1s = [AL.buf(BF16, 2 * CH, f"w1s{i}") for i in range(3)]
        w2sl = [AL.buf(BF16, 512, f"w2sl{i}") for i in range(4)]
        gel = [AL.buf(BF16, CK, f"gel{i}") for i in range(2)]
        w1r = w1[l].rearrange("(k p) n -> p k n", p=128)
        for cc in range(2):
            c0 = cc * 1024
            vop(lambda eng: eng.memset(h3[21].ap(), 0.0), writes=(h3[21].key,))
            for j in range(NJ):
                ws1 = w1s[j % 3]
                M = 128 if j < 21 else 42
                dma("poolq",
                    ws1.ap(0, 128, 0, 1024).rearrange("p (k n) -> p k n", n=128)[:, :, 0:M],
                    w1r[:, :, j * 128:j * 128 + M], (), (ws1.key,))
                dma("poolq",
                    ws1.ap(0, 128, 1024, 2048).rearrange("p (k n) -> p k n", n=128)[:, :, 0:M],
                    w1r[:, :, HID + j * 128:HID + j * 128 + M], (), (ws1.key,))
                ba = [0, 1] if j % 2 == 0 else [4, 5]
                bg = [2, 3] if j % 2 == 0 else [6, 7]
                mm_kmajor(
                    8,
                    lambda k, ws1=ws1, M=M: ws1.ap(0, 128, k * 128, k * 128 + M),
                    lambda k, sub, c0=c0: hb[k].ap(0, 128, c0 + sub * CK,
                                                   c0 + (sub + 1) * CK),
                    ba,
                    lambda k, ws1=ws1: (ws1.key,) + tuple(h.key for h in hb),
                    lambda sub, bk: None, pslice=(0, 128 if j < 21 else 42))
                mm_kmajor(
                    8,
                    lambda k, ws1=ws1, M=M: ws1.ap(0, 128, 1024 + k * 128,
                                                   1024 + k * 128 + M),
                    lambda k, sub, c0=c0: hb[k].ap(0, 128, c0 + sub * CK,
                                                   c0 + (sub + 1) * CK),
                    bg,
                    lambda k, ws1=ws1: (ws1.key,) + tuple(h.key for h in hb),
                    lambda sub, bk: None, pslice=(0, 128 if j < 21 else 42))
                for sub in range(2):
                    gl = gel[sub]
                    vop(lambda eng, M=M, bkg=bg[sub], j=j, l=l, gl=gl: eng.activation(
                        gl.ap(0, M, 0, CK), bank(bg[sub] if False else bkg, 0, M),
                        AF.Gelu, bias=b1g_pp[l].ap(0, M, j, j + 1)),
                        reads=(f"ps{bg[sub]}", b1g_pp[l].key), writes=(gl.key,),
                        eng="act")
                    vop(lambda eng, M=M, bka=ba[sub], j=j, l=l, sub=sub, gl=gl:
                        eng.scalar_tensor_tensor(
                            h3[j].ap(0, M, sub * CK, (sub + 1) * CK),
                            bank(bka, 0, M), b1a_pp[l].ap(0, M, j, j + 1),
                            gl.ap(0, M, 0, CK), OP.add, OP.mult),
                        reads=(f"ps{ba[sub]}", b1a_pp[l].key, gl.key),
                        writes=(h3[j].key,))
            for half in range(2):
                w2g = [Group(f"ps{bi}", inc_each=True) for bi in range(8)]
                for j in range(NJ):
                    ws2 = w2sl[j % 4]
                    Mk = 128 if j < 21 else 42
                    if j == 21:
                        vop(lambda eng, ws2=ws2: eng.memset(ws2.ap(), 0.0),
                            writes=(ws2.key,))
                    dma("poolq", ws2.ap(0, Mk, 0, 512),
                        w2[l, j * 128:j * 128 + Mk, half * 512:(half + 1) * 512],
                        (), (ws2.key,))
                    for hti in range(4):
                        for sub in range(2):
                            bi = hti * 2 + sub
                            w2g[bi].member(
                                bank(bi),
                                ws2.ap(0, 128, hti * 128, (hti + 1) * 128),
                                h3[j].ap(0, 128, sub * CK, (sub + 1) * CK),
                                (ws2.key, h3[j].key), last=(j == NJ - 1))
                for hti in range(4):
                    ht = half * 4 + hti
                    for sub in range(2):
                        bi = hti * 2 + sub
                        vop(lambda eng, ht=ht, c0=c0, sub=sub, bi=bi, l=l:
                            eng.scalar_tensor_tensor(
                                x_cm[ht].ap(0, 128, 1 + c0 + sub * CK,
                                            1 + c0 + (sub + 1) * CK), bank(bi),
                                b2_pp[l].ap(0, 128, ht, ht + 1),
                                x_cm[ht].ap(0, 128, 1 + c0 + sub * CK,
                                            1 + c0 + (sub + 1) * CK),
                                OP.add, OP.add),
                            reads=(f"ps{bi}", b2_pp[l].key, f"x{ht}"),
                            writes=(f"x{ht}",))
        AL.release()
        P.epoch()

    def final_ln_out(b):
        AL.mark()
        ld = [AL.buf(F32, S, f"rb{i}") for i in range(2)]
        for t in range(NT):
            L = ld[t % 2]
            dma("syncq", L.ap(), x1sp[t, :, :], ("x1sp",), (L.key,))
            vop(lambda eng, t=t, L=L: eng.tensor_tensor(
                x_cm[t].ap(0, 128, 1, S + 1), x_cm[t].ap(0, 128, 1, S + 1),
                L.ap(), OP.add),
                reads=(f"x{t}", L.key), writes=(f"x{t}",))
        AL.release()
        AL.mark()
        CST[0] = [AL.buf(BF16, CK, "cst0"), AL.buf(BF16, CK, "cst1")]
        for c in range(NC4):
            stats_chunk(c, None,
                        lambda t, c=c: x_cm[t].ap(0, 128, 1 + c * CK, 1 + (c + 1) * CK))
            bcast_rows()
            for t in range(NT):
                vop(lambda eng, t=t, c=c: eng.tensor_tensor(
                    x_cm[t].ap(0, 128, 1 + c * CK, 1 + (c + 1) * CK),
                    x_cm[t].ap(0, 128, 1 + c * CK, 1 + (c + 1) * CK),
                    bank(5), OP.mult),
                    reads=(f"x{t}", "ps5"), writes=(f"x{t}",))
                vop(lambda eng, t=t, c=c: eng.tensor_tensor(
                    x_cm[t].ap(0, 128, 1 + c * CK, 1 + (c + 1) * CK),
                    x_cm[t].ap(0, 128, 1 + c * CK, 1 + (c + 1) * CK),
                    bank(6), OP.subtract),
                    reads=(f"x{t}", "ps6"), writes=(f"x{t}",))
        for t in range(NT):
            dma("syncq", y_out[b, t * 128:(t + 1) * 128, :],
                x_cm[t].ap(0, 128, 1, S + 1), (f"x{t}",), (f"y{b}_{t}",))
        AL.release()
        P.epoch()

    for b in range(NB):
        load_x(b)
        ssm(b)
        for t in range(NT):
            dma("syncq", x1sp[t, :, :], x_cm[t].ap(0, 128, 1, S + 1),
                (f"x{t}",), ("x1sp",))
        for l in range(2):
            ffn_layer(b, l)
        final_ln_out(b)

    fin = {a: P.count[a] for a in ACCOUNTS if P.count[a]}
    P.ops["syncq"].append((fin, lambda eng: None, 0, "sq0"))

    with nc.Block() as block:
        @block.tensor
        def _(eng):
            P.replay("te", eng, sems)

        @block.vector
        def _(eng):
            P.replay("dve", eng, sems)

        @block.scalar
        def _(eng):
            P.replay("act", eng, sems)

        @block.sync
        def _(eng):
            P.replay("syncq", eng, sems)

        @block.gpsimd
        def _(eng):
            P.replay("poolq", eng, sems)

    ctx.close()
    return nc


_NC_CACHE = [None]


def kernel(L0_attn_x, L0_attn_Wf, L0_attn_Wi, L0_attn_Wv, L0_attn_Wo,
           L0_ffn_wqkv, L0_ffn_q_scale, L0_ffn_k_scale, L0_ffn_wo,
           L0_ffn_ln_g, L0_ffn_ln_b, L0_ffn_w1, L0_ffn_b1, L0_ffn_w2,
           L0_ffn_b2):
    x = np.ascontiguousarray(np.asarray(L0_attn_x, dtype=np.float32))
    cast = lambda a: np.ascontiguousarray(np.asarray(a, dtype=np.float32))
    tables = _mk_tables()
    e2h = np.zeros((128, 2), np.float32)
    e2h[0:64, 0] = 1.0
    e2h[64:128, 1] = 1.0
    ebh = np.zeros((2, 128), np.float32)
    ebh[0, 0:64] = 1.0
    ebh[1, 64:128] = 1.0
    castT = lambda a: np.ascontiguousarray(np.asarray(a, dtype=np.float32).T)
    common = dict(
        Wf=castT(L0_attn_Wf), Wi=castT(L0_attn_Wi), Wv=castT(L0_attn_Wv),
        Wo=castT(L0_attn_Wo), wqkv=cast(L0_ffn_wqkv), wof=cast(L0_ffn_wo),
        w1=cast(L0_ffn_w1), w2=cast(L0_ffn_w2), qsc=cast(L0_ffn_q_scale),
        ksc=cast(L0_ffn_k_scale), lng=cast(L0_ffn_ln_g), lnb=cast(L0_ffn_ln_b),
        b1=cast(L0_ffn_b1), b2=cast(L0_ffn_b2),
        masks=_mk_masks(),
        idf=np.eye(128, dtype=np.float32),
        idb=np.eye(128, dtype=np.float32).astype(BFD),
        prot=_mk_prot().astype(BFD),
        onec=np.ones((128, 1), np.float32).astype(BFD),
        oner=np.ones((1, 128), np.float32),
        e2=e2h.astype(BFD),
        eb=ebh,
        **tables,
    )
    if _NC_CACHE[0] is None:
        _NC_CACHE[0] = build_program()
    nc = _NC_CACHE[0]
    in_maps = []
    for core in range(8):
        m = dict(common)
        m["x"] = np.ascontiguousarray(x[2 * core:2 * core + 2].transpose(0, 2, 1))
        in_maps.append(m)
    res = run_bass_kernel_spmd(nc, in_maps, core_ids=list(range(8)))
    return np.ascontiguousarray(
        np.concatenate([r["y"] for r in res.results], axis=0).transpose(0, 2, 1))


# version 16
# speedup vs baseline: 1.0411x; 1.0411x over previous
from contextlib import ExitStack

import numpy as np
import ml_dtypes

import concourse.bass as bass
import concourse.mybir as mybir
from concourse.bass_utils import run_bass_kernel_spmd

F32 = mybir.dt.float32
BF16 = mybir.dt.bfloat16
AX = mybir.AxisListType.X
AF = mybir.ActivationFunctionType
OP = mybir.AluOpType
BFD = ml_dtypes.bfloat16

S = 2048
CH = 1024
NT = 8
NB = 2
CK = 512
NC4 = 4
HID = 2730
NJ = 22
DH = 64
WIN = 512
NEG = np.float32(-3.0e38)

STREAMS = ("te", "dve", "act", "syncq", "poolq")
NQC = 4
ACCOUNTS = ("te", "dve", "act", "plc") + tuple(f"sq{i}" for i in range(NQC)) \
    + tuple(f"pq{i}" for i in range(NQC))
QCLS = {"syncq": "sq", "poolq": "pq"}

F32_COLS = 27360
BF_COLS = 51680


class Plan:

    def __init__(self):
        self.ops = {e: [] for e in STREAMS}
        self.count = {a: 0 for a in ACCOUNTS}
        self.qidx = {q: 0 for q in QCLS}
        self.last_write = {}
        self.readers = {}
        self.watermark = {a: {} for a in ACCOUNTS}
        self.epoch_marks = {a: 0 for a in ACCOUNTS}

    def waits_for(self, stream, account, reads, writes):
        waits = {}

        def need(src, cnt):
            if cnt <= 0:
                return
            if src == account and src not in ("te", "dve", "act"):
                return
            if cnt > self.watermark[account].get(src, 0) and \
                    cnt > waits.get(src, 0):
                waits[src] = cnt

        for k in reads:
            w = self.last_write.get(k)
            if w:
                need(*w)
        for k in writes:
            w = self.last_write.get(k)
            if w:
                need(*w)
            for src, cnt in self.readers.get(k, {}).items():
                need(src, cnt)
        if writes:
            for src, cnt in self.epoch_marks.items():
                need(src, cnt)
        return waits

    def emit(self, stream, account, fn, waits, inc):
        for src, cnt in waits.items():
            if cnt > self.watermark[account].get(src, 0):
                self.watermark[account][src] = cnt
        self.count[account] += inc
        self.ops[stream].append((waits, fn, inc, account))
        return self.count[account]

    def record(self, account, cnt, reads, writes):
        for k in reads:
            self.readers.setdefault(k, {})[account] = cnt
        for k in writes:
            self.last_write[k] = (account, cnt)
            self.readers[k] = {}

    def op(self, stream, fn, reads=(), writes=(), inc=1):
        if stream in QCLS:
            i = self.qidx[stream]
            self.qidx[stream] = i + 1
            account = f"{QCLS[stream]}{i % NQC}"
            waits = self.waits_for(stream, account, reads, writes)
            prior = 16 * (i // NQC)
            if prior > 0:
                waits[account] = prior
            cnt = self.emit(stream, account, fn, waits, 16)
            self.record(account, cnt, reads, writes)
        else:
            account = stream
            waits = self.waits_for(stream, account, reads, writes)
            cnt = self.emit(stream, account, fn, waits, inc)
            self.record(account, cnt, reads, writes)

    def epoch(self):
        self.epoch_marks = dict(self.count)

    def replay(self, name, eng, sems):
        for waits, fn, inc, account in self.ops[name]:
            for src in sorted(waits):
                eng.wait_ge(sems[src], waits[src])
            ins = fn(eng)
            if inc and ins is not None:
                ins.then_inc(sems[account], inc)


class Region:
    def __init__(self, tensor, off, cols, name):
        self.tensor = tensor
        self.off = off
        self.cols = cols
        self.key = name

    def ap(self, p0=0, p1=128, c0=0, c1=None):
        c1 = self.cols if c1 is None else c1
        assert 0 <= c0 <= c1 <= self.cols, (self.key, c0, c1, self.cols)
        return self.tensor[p0:p1, self.off + c0:self.off + c1]


class Alloc:
    def __init__(self, tensors, sizes):
        self.t = tensors
        self.size = sizes
        self.off = {k: 0 for k in tensors}
        self.marks = []
        self.peak = {k: 0 for k in tensors}
        self.on_release = None

    def buf(self, dtype, cols, name):
        o = (self.off[dtype] + 15) // 16 * 16
        self.off[dtype] = o + cols
        self.peak[dtype] = max(self.peak[dtype], self.off[dtype])
        assert self.off[dtype] <= self.size[dtype], (name, dtype, self.off[dtype])
        return Region(self.t[dtype], o, cols, name)

    def mark(self):
        self.marks.append(dict(self.off))

    def release(self):
        self.off = self.marks.pop()
        if self.on_release is not None:
            self.on_release()


def _mk_tables():
    half = np.arange(0, DH, 2, dtype=np.float64)
    inv_freq = 1.0 / 10000.0 ** (half / DH)
    t = np.arange(S, dtype=np.float64)
    freqs = t[:, None] * inv_freq[None, :]
    freqs = np.concatenate([freqs, freqs], axis=1)
    base = (half + 0.4 * DH) / (1.4 * DH)
    base = np.concatenate([base, base])
    power = (t - S // 2) / float(WIN // 2)
    scale = base[None, :] ** power[:, None]
    cos, sin = np.cos(freqs), np.sin(freqs)
    sign = np.ones(DH)
    sign[DH // 2:] = -1.0
    out = {}
    for nm, tab in (("csq", 8.0 * cos * scale), ("ssq", 8.0 * sin * scale * sign),
                    ("csk", cos / scale), ("ssk", sin / scale * sign)):
        tt = np.ascontiguousarray(tab.T)
        tt = np.concatenate([tt, tt], axis=0)
        hi = tt.astype(np.float32)
        lo = (tt - hi.astype(np.float64)).astype(np.float32)
        out[nm + "_hi"] = np.ascontiguousarray(hi)
        out[nm + "_lo"] = np.ascontiguousarray(lo)
    return out


def _mk_masks():
    m = np.zeros((2, 128, 128), np.float32)
    i = np.arange(128)[:, None]
    j = np.arange(128)[None, :]
    m[0] = np.where(j >= i, 0.0, NEG)
    m[1] = np.where(j <= i, 0.0, NEG)
    return m


def _mk_prot():
    p = np.zeros((128, 128), np.float32)
    for mcol in range(128):
        partner = mcol + 32 if (mcol % 64) < 32 else mcol - 32
        p[partner, mcol] = 1.0
    return p


def build_program():
    nc = bass.Bass("TRN2", target_bir_lowering=False)

    def din(name, shape, dt=F32):
        return nc.dram_tensor(name, shape, dt, kind="ExternalInput").ap()

    x_in = din("x", [NB, CH, S])
    Wg = [din(n, [CH, CH]) for n in ("Wf", "Wi", "Wv")]
    Wo = din("Wo", [CH, CH])
    wqkv = din("wqkv", [2, CH, 1536])
    wof = din("wof", [2, 512, CH])
    w1 = din("w1", [2, CH, 2 * HID])
    w2 = din("w2", [2, HID, CH])
    qsc = din("qsc", [2, DH])
    ksc = din("ksc", [2, DH])
    lng = din("lng", [2, CH])
    lnb = din("lnb", [2, CH])
    b1 = din("b1", [2, 2 * HID])
    b2 = din("b2", [2, CH])
    masks_in = din("masks", [2, 128, 128])
    tabs_in = {nm: din(nm, [128, S]) for nm in
               ("csq_hi", "csq_lo", "ssq_hi", "ssq_lo",
                "csk_hi", "csk_lo", "ssk_hi", "ssk_lo")}
    idf_in = din("idf", [128, 128])
    idb_in = din("idb", [128, 128], BF16)
    prot_in = din("prot", [128, 128], BF16)
    onec_in = din("onec", [128, 1], BF16)
    onecf_in = din("onecf", [128, 1])
    e2f_in = din("e2f", [128, 2])
    oner_in = din("oner", [1, 128])
    e2_in = din("e2", [128, 2], BF16)
    eb_in = din("eb", [2, 128])

    y_out = nc.dram_tensor("y", [NB, CH, S], F32, kind="ExternalOutput").ap()
    x1sp = nc.dram_tensor("x1sp", [NT, 128, S], F32, kind="Internal").ap()

    P = Plan()
    ctx = ExitStack()
    a32 = ctx.enter_context(nc.sbuf_tensor("a32", [128, F32_COLS], F32))
    a16 = ctx.enter_context(nc.sbuf_tensor("a16", [128, BF_COLS], BF16))
    AL = Alloc({F32: a32, BF16: a16}, {F32: F32_COLS, BF16: BF_COLS})
    AL.on_release = lambda: P.epoch()
    psum = ctx.enter_context(nc.psum_tensor("ps", [128, 4096], F32))
    sems = {a: ctx.enter_context(nc.semaphore("sem_" + a)) for a in ACCOUNTS}

    def bank(i, p0=0, p1=128, c0=0, c1=512):
        assert 0 <= c0 <= c1 <= 512
        return psum[p0:p1, i * 512 + c0:i * 512 + c1]

    ident_f = AL.buf(F32, 128, "identf")
    oner = AL.buf(F32, 128, "oner")
    eb = AL.buf(F32, 128, "eb")
    rowM = AL.buf(F32, CK, "rowM")
    rowV = AL.buf(F32, CK, "rowV")
    rowI = AL.buf(F32, CK, "rowI")
    rowR = AL.buf(F32, CK, "rowR")
    rowT = AL.buf(F32, S, "rowT")
    rql = AL.buf(F32, 32, "rql")
    x_cm = [AL.buf(F32, S + 1, f"x{t}") for t in range(NT)]
    ident_b = AL.buf(BF16, 128, "identb")
    prot = AL.buf(BF16, 128, "prot")
    onec = AL.buf(BF16, 16, "onec")
    onecf = AL.buf(F32, 1, "onecf")
    e2f = AL.buf(F32, 2, "e2f")
    e2 = AL.buf(BF16, 16, "e2")
    prm = AL.buf(F32, 144, "prm")
    def _sub(l, off, cols):
        return Region(prm.tensor, prm.off + l * 72 + off, cols, "prm")
    g_pp = [_sub(l, 0, 8) for l in range(2)]
    bb_pp = [_sub(l, 8, 8) for l in range(2)]
    qs_pp = [_sub(l, 16, 1) for l in range(2)]
    ks_pp = [_sub(l, 17, 1) for l in range(2)]
    b1a_pp = [_sub(l, 18, NJ) for l in range(2)]
    b1g_pp = [_sub(l, 40, NJ) for l in range(2)]
    b2_pp = [_sub(l, 62, 8) for l in range(2)]
    CST = [None]
    TABS = [None]
    MASKS = [None]

    def dma(q, out_ap, in_ap, reads=(), writes=(), slow=False):
        P.op(q, (lambda eng, o=out_ap, i=in_ap, s=slow: eng.dma_start(
            out=o, in_=i, allow_slow_non_contiguous=s)),
             reads=reads, writes=writes)

    class Group:

        def __init__(self, bank_key, inc_each=False):
            self.bank_key = bank_key
            self.inc_each = inc_each
            self.first = True

        def member(self, o, lt, r, rd, last=False):
            w = (self.bank_key,) if self.first else ()
            waits = P.waits_for("te", "te", rd, w)
            st = self.first
            self.first = False
            sp = last
            inc = 1 if (self.inc_each or last) else 0
            cnt = P.emit("te", "te", (lambda eng, o=o, lt=lt, r=r, st=st, sp=sp:
                                      (eng.matmul(o, lt, r, start=st, stop=sp),
                                       None)[1] if not (sp or True) else
                                      eng.matmul(o, lt, r, start=st, stop=sp)),
                         waits, inc)
            P.record("te", cnt if inc else P.count["te"] + 1, rd,
                     (self.bank_key,) if last else ())

    def mm_members(members, bank_key):
        g = Group(bank_key)
        for i, (o, lt, r, rd) in enumerate(members):
            g.member(o, lt, r, rd, last=(i == len(members) - 1))

    def mmg(items, reads, writes):
        mm_members([(o, lt, r, reads if i == 0 else ())
                    for i, (o, lt, r) in enumerate(items)], writes[0])

    def mm_kmajor(nk, lhsT_of_k, rhs_of_kc, banks, reads_of_k, evac,
                  pslice=(0, 128)):
        first = [True] * len(banks)
        p0, p1 = pslice
        for k in range(nk):
            for ci, bk in enumerate(banks):
                rd = reads_of_k(k) if (ci == 0 or k == nk - 1) else ()
                w = (f"ps{bk}",) if first[ci] else ()
                waits = P.waits_for("te", "te", rd, w)
                st = first[ci]
                first[ci] = False
                sp = (k == nk - 1)
                inc = 1 if sp else 0
                lt = lhsT_of_k(k)
                r = rhs_of_kc(k, ci)
                cnt = P.emit("te", "te",
                             (lambda eng, lt=lt, r=r, st=st, sp=sp, bk=bk:
                              eng.matmul(bank(bk, p0, p1), lt, r, start=st, stop=sp)),
                             waits, inc)
                P.record("te", cnt if inc else P.count["te"] + 1, rd,
                         (f"ps{bk}",) if sp else ())
        for ci, bk in enumerate(banks):
            if evac is not None:
                evac(ci, bk)

    def mm1(out_ap, lt_ap, r_ap, reads, bank_key):
        P.op("te", (lambda eng, o=out_ap, lt=lt_ap, r=r_ap:
                    eng.matmul(o, lt, r, start=True, stop=True)),
             reads=reads, writes=(bank_key,), inc=1)

    def vop(fn, reads=(), writes=(), eng="dve"):
        if eng == "poolc":
            waits = P.waits_for("poolq", "plc", reads, writes)
            cnt = P.emit("poolq", "plc", fn, waits, 1)
            P.record("plc", cnt, reads, writes)
        else:
            P.op(eng, fn, reads=reads, writes=writes, inc=1)

    dma("syncq", ident_f.ap(), idf_in[:, :], (), ("identf",))
    dma("syncq", oner.ap(0, 1, 0, 128), oner_in[:, :], (), ("oner",))
    dma("syncq", eb.ap(0, 2, 0, 128), eb_in[:, :], (), ("eb",))
    dma("syncq", eb.ap(32, 34, 0, 128), eb_in[:, :], (), ("eb",))
    dma("syncq", ident_b.ap(), idb_in[:, :], (), ("identb",))
    dma("syncq", prot.ap(), prot_in[:, :], (), ("prot",))
    dma("syncq", onec.ap(0, 128, 0, 1), onec_in[:, :], (), ("onec",))
    dma("syncq", onecf.ap(0, 128, 0, 1), onecf_in[:, :], (), ("onecf",))
    dma("syncq", e2f.ap(0, 128, 0, 2), e2f_in[:, :], (), ("e2f",))
    dma("syncq", e2.ap(0, 128, 0, 2), e2_in[:, :], (), ("e2",))
    for l in range(2):
        dma("syncq", g_pp[l].ap(), lng[l, :].rearrange("(t p) -> p t", p=128),
            (), (g_pp[l].key,), slow=True)
        dma("syncq", bb_pp[l].ap(), lnb[l, :].rearrange("(t p) -> p t", p=128),
            (), (bb_pp[l].key,), slow=True)
        dma("syncq", qs_pp[l].ap(0, 64), qsc[l:l+1, :].transpose([1, 0]),
            (), (qs_pp[l].key,), slow=True)
        dma("syncq", qs_pp[l].ap(64, 128), qsc[l:l+1, :].transpose([1, 0]),
            (), (qs_pp[l].key,), slow=True)
        dma("syncq", ks_pp[l].ap(0, 64), ksc[l:l+1, :].transpose([1, 0]),
            (), (ks_pp[l].key,), slow=True)
        dma("syncq", ks_pp[l].ap(64, 128), ksc[l:l+1, :].transpose([1, 0]),
            (), (ks_pp[l].key,), slow=True)
        dma("syncq", b1a_pp[l].ap(0, 128, 0, 21),
            b1[l, 0:2688].rearrange("(j p) -> p j", p=128), (), (b1a_pp[l].key,), slow=True)
        dma("syncq", b1a_pp[l].ap(0, 42, 21, 22),
            b1[l:l+1, 2688:2730].transpose([1, 0]), (), (b1a_pp[l].key,), slow=True)
        dma("syncq", b1g_pp[l].ap(0, 128, 0, 21),
            b1[l, HID:HID + 2688].rearrange("(j p) -> p j", p=128), (), (b1g_pp[l].key,), slow=True)
        dma("syncq", b1g_pp[l].ap(0, 42, 21, 22),
            b1[l:l+1, HID + 2688:2 * HID].transpose([1, 0]), (), (b1g_pp[l].key,), slow=True)
        dma("syncq", b2_pp[l].ap(), b2[l, :].rearrange("(t p) -> p t", p=128),
            (), (b2_pp[l].key,), slow=True)
    for t in range(NT):
        vop(lambda eng, t=t: eng.memset(x_cm[t].ap(0, 128, 0, 1), 0.0),
            writes=(x_cm[t].key,))
    P.epoch()

    bmi = [0]

    def nbm():
        b = bmi[0] % 4
        bmi[0] += 1
        return b

    def stats_chunk(c, sum_tiles, xsrc, sqt=None):
        mems = [(bank(7, 0, 1, 0, 512), onecf.ap(0, 128, 0, 1),
                 xsrc(t), (f"x{t}", "onecf")) for t in range(NT)]
        mm_members(mems, "ps7")
        vop(lambda eng: eng.tensor_scalar(rowM.ap(0, 1), bank(7, 0, 1, 0, 512),
                                          1.0 / CH, None, OP.mult),
            reads=("ps7",), writes=("rowM",))
        g = Group("ps7", inc_each=True)
        for t in range(NT):
            cb = sqt[t % 2]
            vop(lambda eng, t=t, cb=cb: eng.tensor_tensor(
                cb.ap(), xsrc(t), xsrc(t), OP.mult),
                reads=(f"x{t}",), writes=(cb.key,), eng="poolc")
            g.member(bank(7, 32, 33, 0, 512), onecf.ap(0, 128, 0, 1),
                     cb.ap(), (cb.key, "onecf"), last=(t == NT - 1))
        vop(lambda eng: eng.activation(rowV.ap(0, 1), rowM.ap(0, 1), AF.Square),
            reads=("rowM",), writes=("rowV",), eng="act")
        vop(lambda eng: eng.tensor_scalar(rowV.ap(0, 1), rowV.ap(0, 1),
                                          1e-5, None, OP.subtract),
            reads=("rowV",), writes=("rowV",))
        vop(lambda eng: eng.scalar_tensor_tensor(
            rowV.ap(0, 1), bank(7, 32, 33, 0, 512), 1.0 / CH, rowV.ap(0, 1),
            OP.mult, OP.subtract),
            reads=("ps7", "rowV"), writes=("rowV",))
        vop(lambda eng: eng.activation(rowV.ap(0, 1), rowV.ap(0, 1), AF.Sqrt),
            reads=("rowV",), writes=("rowV",), eng="act")
        vop(lambda eng: eng.reciprocal(rowI.ap(0, 1), rowV.ap(0, 1)),
            reads=("rowV",), writes=("rowI",))
        vop(lambda eng: eng.tensor_tensor(rowR.ap(0, 1), rowM.ap(0, 1),
                                          rowI.ap(0, 1), OP.mult),
            reads=("rowM", "rowI"), writes=("rowR",))

    def bcast_rows():
        mm1(bank(5), oner.ap(0, 1, 0, 128), rowI.ap(0, 1, 0, 512),
            ("rowI", "oner"), "ps5")
        mm1(bank(6), oner.ap(0, 1, 0, 128), rowR.ap(0, 1, 0, 512),
            ("rowR", "oner"), "ps6")

    def outer_ln(sqt):
        for c in range(NC4):
            stats_chunk(c, None,
                        lambda t, c=c: x_cm[t].ap(0, 128, 1 + c * CK, 1 + (c + 1) * CK),
                        sqt)
            bcast_rows()
            for t in range(NT):
                vop(lambda eng, t=t, c=c: eng.tensor_tensor(
                    x_cm[t].ap(0, 128, 1 + c * CK, 1 + (c + 1) * CK),
                    x_cm[t].ap(0, 128, 1 + c * CK, 1 + (c + 1) * CK),
                    bank(5), OP.mult),
                    reads=(f"x{t}", "ps5"), writes=(f"x{t}",))
                vop(lambda eng, t=t, c=c: eng.tensor_tensor(
                    x_cm[t].ap(0, 128, 1 + c * CK, 1 + (c + 1) * CK),
                    x_cm[t].ap(0, 128, 1 + c * CK, 1 + (c + 1) * CK),
                    bank(6), OP.subtract),
                    reads=(f"x{t}", "ps6"), writes=(f"x{t}",))

    def load_x(b):
        for t in range(NT):
            dma("syncq", x_cm[t].ap(0, 128, 1, S + 1),
                x_in[b, t * 128:(t + 1) * 128, :], (), (f"x{t}",))
        P.epoch()

    def ssm(b):
        AL.mark()
        x_bf = [AL.buf(BF16, S, f"xb{t}") for t in range(NT)]
        s_bf = [AL.buf(BF16, S, f"sb{t}") for t in range(NT)]
        fga = AL.buf(F32, S, "fga")
        fgb = AL.buf(F32, S, "fgb")
        fgc = AL.buf(F32, S, "fgc")
        wsl = [AL.buf(BF16, 3 * CH, f"wsl{i}") for i in range(2)]
        for t in range(NT):
            vop(lambda eng, t=t: eng.copy(x_bf[t].ap(), x_cm[t].ap(0, 128, 1, S + 1)),
                reads=(f"x{t}",), writes=(x_bf[t].key,), eng="act")
        actf = (AF.Tanh, AF.Sigmoid, AF.Silu)
        outs = (fga, fgb, fgc)
        WgT = [Wg[gi].rearrange("(k p) n -> p k n", p=128) for gi in range(3)]
        WoT = Wo.rearrange("(k p) n -> p k n", p=128)
        for nt in range(NT):
            tr = wsl[nt % 2]
            for gi in range(3):
                dma("poolq", tr.ap(0, 128, gi * CH, (gi + 1) * CH).rearrange(
                    "p (k n) -> p k n", n=128),
                    WgT[gi][:, :, nt * 128:(nt + 1) * 128], (), (tr.key,))
            for gi in range(3):
                mm_kmajor(
                    8,
                    lambda k, gi=gi, tr=tr: tr.ap(0, 128, gi * CH + k * 128,
                                                  gi * CH + (k + 1) * 128),
                    lambda k, c: x_bf[k].ap(0, 128, c * CK, (c + 1) * CK),
                    [0, 1, 2, 3] if gi % 2 == 0 else [4, 5, 6, 7],
                    lambda k, tr=tr: (tr.key,) + tuple(x.key for x in x_bf),
                    lambda c, bk, gi=gi: vop(lambda eng, gi=gi, c=c, bk=bk: eng.activation(
                        outs[gi].ap(0, 128, c * CK, (c + 1) * CK), bank(bk), actf[gi]),
                        reads=(f"ps{bk}",), writes=(outs[gi].key,), eng="act"))
            vop(lambda eng: eng.tensor_tensor(fgb.ap(), fgb.ap(), fgc.ap(), OP.mult),
                reads=("fgb", "fgc"), writes=("fgb",))
            vop(lambda eng: eng.tensor_tensor_scan(fgc.ap(), fga.ap(), fgb.ap(),
                                                   0.0, OP.mult, OP.add),
                reads=("fga", "fgb"), writes=("fgc",))
            vop(lambda eng, nt=nt: eng.activation(s_bf[nt].ap(), fgc.ap(), AF.Silu),
                reads=("fgc",), writes=(s_bf[nt].key,), eng="act")
        for ht in range(NT):
            tr = wsl[ht % 2]
            dma("poolq", tr.ap(0, 128, 0, CH).rearrange("p (k n) -> p k n", n=128),
                WoT[:, :, ht * 128:(ht + 1) * 128], (), (tr.key,))
            mm_kmajor(
                8,
                lambda k, tr=tr: tr.ap(0, 128, k * 128, (k + 1) * 128),
                lambda k, c: s_bf[k].ap(0, 128, c * CK, (c + 1) * CK),
                [0, 1, 2, 3] if ht % 2 == 0 else [4, 5, 6, 7],
                lambda k, tr=tr: (tr.key,) + tuple(s.key for s in s_bf),
                lambda c, bk, ht=ht: vop(lambda eng, ht=ht, c=c, bk=bk: eng.tensor_tensor(
                    x_cm[ht].ap(0, 128, 1 + c * CK, 1 + (c + 1) * CK),
                    x_cm[ht].ap(0, 128, 1 + c * CK, 1 + (c + 1) * CK),
                    bank(bk), OP.add),
                    reads=(f"ps{bk}", f"x{ht}"), writes=(f"x{ht}",)))
        sqt = [Region(fga.tensor, fga.off, CK, "sqf0"),
               Region(fgb.tensor, fgb.off, CK, "sqf1")]
        outer_ln(sqt)
        AL.release()
        P.epoch()

    def proc_qk(l, qr, sqb, rota, rotb, cs, ss, spp, dst, par):
        pr0 = 32 * par
        rkey = f"rowT{par}"
        sbk = 7 if par == 0 else 4
        for c in range(NC4):
            vop(lambda eng, c=c: eng.tensor_tensor(
                rota.ap(), qr.ap(0, 128, c * CK, (c + 1) * CK),
                qr.ap(0, 128, c * CK, (c + 1) * CK), OP.mult),
                reads=(qr.key,), writes=(rota.key,), eng="poolc")
            mm1(bank(sbk, 0, 2, 0, 512), e2f.ap(0, 128, 0, 2),
                rota.ap(), (rota.key, "e2f"), f"ps{sbk}")
            vop(lambda eng, c=c: eng.tensor_scalar(
                rowT.ap(pr0, pr0 + 2, c * CK, (c + 1) * CK), bank(sbk, 0, 2, 0, 512),
                1e-24, None, OP.max),
                reads=(f"ps{sbk}",), writes=(rkey,))
        vop(lambda eng: eng.activation(rowT.ap(pr0, pr0 + 2), rowT.ap(pr0, pr0 + 2),
                                       AF.Sqrt),
            reads=(rkey,), writes=(rkey,), eng="act")
        vop(lambda eng: eng.reciprocal(rowT.ap(pr0, pr0 + 2), rowT.ap(pr0, pr0 + 2)),
            reads=(rkey,), writes=(rkey,))
        rbk = 6 if par == 0 else 5
        pbk = 5 if par == 0 else 6
        for c in range(NC4):
            mm1(bank(rbk), eb.ap(pr0, pr0 + 2, 0, 128),
                rowT.ap(pr0, pr0 + 2, c * CK, (c + 1) * CK),
                (rkey, "eb"), f"ps{rbk}")
            vop(lambda eng, c=c: eng.tensor_tensor(
                qr.ap(0, 128, c * CK, (c + 1) * CK),
                qr.ap(0, 128, c * CK, (c + 1) * CK), bank(rbk), OP.mult),
                reads=(qr.key, f"ps{rbk}"), writes=(qr.key,))
            vop(lambda eng, c=c: eng.scalar_tensor_tensor(
                rota.ap(), qr.ap(0, 128, c * CK, (c + 1) * CK), spp.ap(0, 128, 0, 1),
                cs.ap(0, 128, c * CK, (c + 1) * CK), OP.mult, OP.mult),
                reads=(qr.key, spp.key, cs.key), writes=(rota.key,))
            vop(lambda eng, c=c: eng.scalar_tensor_tensor(
                rotb.ap(), qr.ap(0, 128, c * CK, (c + 1) * CK), spp.ap(0, 128, 0, 1),
                ss.ap(0, 128, c * CK, (c + 1) * CK), OP.mult, OP.mult),
                reads=(qr.key, spp.key, ss.key), writes=(rotb.key,))
            mm1(bank(pbk), prot.ap(), rotb.ap(), (rotb.key, "prot"), f"ps{pbk}")
            vop(lambda eng, c=c: eng.tensor_tensor(
                dst.ap(0, 128, c * CK, (c + 1) * CK),
                rota.ap(), bank(pbk), OP.add),
                reads=(rota.key, f"ps{pbk}"), writes=(dst.key,))

    def attention(l, q_bf, k_bf, v_tm, o_bf, sim_f, exf, at_tp, att_t, smax, ssum):
        PAIRS = [(4, 5), (6, 7), (0, 1), (2, 3)]
        units = []
        for w in range(4):
            for ri in range(4):
                qt = w * 4 + ri
                for h in range(8):
                    units.append((w, ri, qt, h))
        avg = {}

        def geo(i):
            w, ri, qt, h = units[i]
            n = 5 if w > 0 else ri + 1
            kb0 = (w - 1) * 4 + ri if w > 0 else 0
            return w, ri, qt, h, n, kb0

        def emit_sim(i):
            w, ri, qt, h, n, kb0 = geo(i)
            th, par = h // 2, h % 2
            p0 = 64 * par
            sA, sB = PAIRS[i % 4]
            q0 = qt * 128
            W = n * 128
            W1 = min(W, 512)
            lt = q_bf[th].ap(p0, p0 + 64, q0, q0 + 128)
            mm1(bank(sA, 0, 128, 0, W1), lt,
                k_bf[th].ap(p0, p0 + 64, kb0 * 128, kb0 * 128 + W1),
                (q_bf[th].key, k_bf[th].key), f"ps{sA}")
            if W > 512:
                mm1(bank(sB, 0, 128, 0, W - 512), lt,
                    k_bf[th].ap(p0, p0 + 64, kb0 * 128 + 512, kb0 * 128 + W),
                    (q_bf[th].key, k_bf[th].key), f"ps{sB}")

        def emit_soft(i):
            w, ri, qt, h, n, kb0 = geo(i)
            sA, sB = PAIRS[i % 4]
            W = n * 128
            sf = sim_f[i % 4]
            ex = exf[i % 4]
            atp = at_tp[i % 4]

            def pch(c):
                if c < 4:
                    return bank(sA, 0, 128, c * 128, (c + 1) * 128), f"ps{sA}"
                return bank(sB, 0, 128, (c - 4) * 128, (c - 3) * 128), f"ps{sB}"
            lastc = n - 1
            if w > 0:
                pap, pkey = pch(0)
                vop(lambda eng, pap=pap, sf=sf: eng.tensor_tensor(
                    sf.ap(0, 128, 0, 128), pap,
                    MASKS[0].ap(0, 128, 0, 128), OP.add),
                    reads=(pkey, "masks"), writes=(sf.key,))
            pap, pkey = pch(lastc)
            vop(lambda eng, pap=pap, sf=sf, lastc=lastc: eng.tensor_tensor(
                sf.ap(0, 128, lastc * 128, (lastc + 1) * 128), pap,
                MASKS[0].ap(0, 128, 128, 256), OP.add),
                reads=(pkey, "masks"), writes=(sf.key,))
            c0m = 1 if w > 0 else 0
            if lastc > c0m:
                nm_ = min(4, lastc) - c0m
                if nm_ > 0:
                    vop(lambda eng, c0m=c0m, nm_=nm_, sA=sA, sf=sf: eng.copy(
                        sf.ap(0, 128, c0m * 128, (c0m + nm_) * 128),
                        bank(sA, 0, 128, c0m * 128, (c0m + nm_) * 128)),
                        reads=(f"ps{sA}",), writes=(sf.key,), eng="act")
                if lastc > 4:
                    vop(lambda eng, sB=sB, sf=sf: eng.copy(
                        sf.ap(0, 128, 512, 640), bank(sB, 0, 128, 0, 128)),
                        reads=(f"ps{sB}",), writes=(sf.key,), eng="act")
            vop(lambda eng, W=W, h=h, sf=sf: eng.reduce_max(
                smax.ap(0, 128, h, h + 1), sf.ap(0, 128, 0, W), axis=AX,
                negate=True),
                reads=(sf.key,), writes=("smax",))
            vop(lambda eng, W=W, h=h, sf=sf, ex=ex: eng.activation(
                ex.ap(0, 128, 0, W), sf.ap(0, 128, 0, W), AF.Exp,
                bias=smax.ap(0, 128, h, h + 1),
                accum_out=ssum.ap(0, 128, h, h + 1)),
                reads=(sf.key, "smax"), writes=(ex.key, "ssum"), eng="act")
            vop(lambda eng, h=h: eng.reciprocal(
                smax.ap(0, 128, 8 + h, 9 + h), ssum.ap(0, 128, h, h + 1)),
                reads=("ssum",), writes=("smaxr",))
            vop(lambda eng, W=W, h=h, atp=atp, ex=ex: eng.activation(
                atp.ap(0, 128, 0, W), ex.ap(0, 128, 0, W), AF.Copy,
                scale=smax.ap(0, 128, 8 + h, 9 + h)),
                reads=(ex.key, "smaxr"), writes=(atp.key,), eng="act")

        def emit_tail(i):
            w, ri, qt, h, n, kb0 = geo(i)
            th, par = h // 2, h % 2
            p0 = 64 * par
            sA, sB = PAIRS[i % 4]
            q0 = qt * 128
            W = n * 128
            W1 = min(W, 512)
            atp = at_tp[i % 4]
            att = att_t[i % 4]
            for c in range(n):
                bk = sA if c < 4 else sB
                cc = (c % 4) * 128
                mm1(bank(bk, 0, 128, cc, cc + 128),
                    atp.ap(0, 128, c * 128, (c + 1) * 128), ident_b.ap(),
                    (atp.key, "identb"), f"ps{bk}")
            vop(lambda eng, W1=W1, att=att, sA=sA: eng.tensor_copy(
                att.ap(0, 128, 0, W1), bank(sA, 0, 128, 0, W1)),
                reads=(f"ps{sA}",), writes=(att.key,))
            if n > 4:
                vop(lambda eng, att=att, sB=sB: eng.copy(
                    att.ap(0, 128, 512, 640), bank(sB, 0, 128, 0, 128)),
                    reads=(f"ps{sB}",), writes=(att.key,), eng="act")
            pe = (i - par) % 4
            avb = PAIRS[pe][1]
            g = Group(f"ps{avb}", inc_each=True)
            for c in range(n):
                g.member(bank(avb, p0, p0 + 64, 384, 512),
                         v_tm[th].ap(0, 128, (kb0 + c) * 128 + p0,
                                     (kb0 + c) * 128 + p0 + 64),
                         att.ap(0, 128, c * 128, (c + 1) * 128),
                         (v_tm[th].key, att.key),
                         last=(c == n - 1))
            if par == 1:
                vop(lambda eng, th=th, q0=q0, avb=avb: eng.tensor_copy(
                    o_bf[th].ap(0, 128, q0, q0 + 128),
                    bank(avb, 0, 128, 384, 512)),
                    reads=(f"ps{avb}",), writes=(o_bf[th].key,))

        NU = len(units)
        for i in range(NU + 2):
            if i < NU:
                emit_sim(i)
            if 1 <= i < NU + 1:
                emit_soft(i - 1)
            if i >= 2:
                emit_tail(i - 2)

    def ffn_layer(b, l):
        AL.mark()
        q_bf = [AL.buf(BF16, S, f"qbf{t}") for t in range(4)]
        k_bf = [AL.buf(BF16, S, f"kbf{t}") for t in range(4)]
        v_tm = [AL.buf(BF16, S, f"vtm{t}") for t in range(4)]

        AL.mark()
        TABS[0] = {nm: AL.buf(BF16, S, "tab" + nm)
                   for nm in ("csq", "ssq", "csk", "ssk")}
        tabs = TABS[0]
        AL.mark()
        thi = AL.buf(F32, S, "thi")
        tlo = AL.buf(F32, S, "tlo")
        for nm in ("csq", "ssq", "csk", "ssk"):
            dma("syncq", thi.ap(), tabs_in[nm + "_hi"][:, :], (), ("thi",))
            dma("syncq", tlo.ap(), tabs_in[nm + "_lo"][:, :], (), ("tlo",))
            vop(lambda eng, nm=nm: eng.tensor_tensor(tabs[nm].ap(), thi.ap(),
                                                     tlo.ap(), OP.add),
                reads=("thi", "tlo"), writes=(tabs[nm].key,))
        AL.release()
        h_bf = [AL.buf(BF16, S, f"hb{t}") for t in range(NT)]
        wqs = [AL.buf(BF16, CH, "wqs0")]
        qraw = [AL.buf(F32, S, "qraw0"), AL.buf(F32, S, "qraw1")]
        vcm = AL.buf(BF16, CK, "vcm")
        _sq = AL.buf(BF16, CK, "sqql")
        sqb = [_sq, _sq]
        rotb = sqb
        rota = [AL.buf(F32, CK, "rota0"), AL.buf(F32, CK, "rota1")]
        for t in range(NT):
            so = 1 if t < 4 else 0
            vop(lambda eng, t=t, so=so: eng.copy(h_bf[t].ap(),
                                                 x_cm[t].ap(0, 128, so, so + S)),
                reads=(f"x{t}",), writes=(h_bf[t].key,), eng="act")
        wq = wqkv[l].rearrange("(k p) n -> p k n", p=128)
        pending = [None]

        def flush_pending():
            if pending[0] is not None:
                m0 = pending[0]
                if m0 < 4:
                    proc_qk(l, qraw[m0 % 2], sqb[m0 % 2], rota[m0 % 2], rotb[m0 % 2],
                            tabs["csq"], tabs["ssq"], qs_pp[l], q_bf[m0], m0 % 2)
                else:
                    proc_qk(l, qraw[m0 % 2], sqb[m0 % 2], rota[m0 % 2], rotb[m0 % 2],
                            tabs["csk"], tabs["ssk"], ks_pp[l], k_bf[m0 - 4], m0 % 2)
                pending[0] = None

        for m in range(12):
            ws = wqs[0]
            dma("poolq", ws.ap().rearrange("p (k n) -> p k n", n=128),
                wq[:, :, m * 128:(m + 1) * 128], (), (ws.key,))

            def qkv_evac(c, bk, m=m):
                if m < 8:
                    qr = qraw[m % 2]
                    vop(lambda eng, c=c, bk=bk, qr=qr: eng.copy(
                        qr.ap(0, 128, c * CK, (c + 1) * CK), bank(bk)),
                        reads=(f"ps{bk}",), writes=(qr.key,), eng="act")
                else:
                    t = m - 8
                    vop(lambda eng, c=c, bk=bk: eng.copy(vcm.ap(), bank(bk)),
                        reads=(f"ps{bk}",), writes=("vcm",), eng="act")
                    for blk in range(4):
                        bk2 = 4 + (blk % 2)
                        mm1(bank(bk2, 0, 128, 0, 128),
                            vcm.ap(0, 128, blk * 128, (blk + 1) * 128),
                            ident_b.ap(), ("vcm", "identb"), f"ps{bk2}")
                        vop(lambda eng, t=t, c=c, blk=blk, bk2=bk2: eng.tensor_copy(
                            v_tm[t].ap(0, 128, (c * 4 + blk) * 128,
                                       (c * 4 + blk + 1) * 128),
                            bank(bk2, 0, 128, 0, 128)),
                            reads=(f"ps{bk2}",), writes=(v_tm[t].key,))
            mm_kmajor(
                8,
                lambda k, ws=ws: ws.ap(0, 128, k * 128, (k + 1) * 128),
                lambda k, c: h_bf[k].ap(0, 128, c * CK, (c + 1) * CK),
                [0, 1, 2, 3],
                lambda k, ws=ws: (ws.key,) + tuple(h.key for h in h_bf),
                qkv_evac)
            flush_pending()
            if m < 8:
                pending[0] = m
        flush_pending()
        AL.release()
        P.epoch()

        AL.mark()
        MASKS[0] = AL.buf(BF16, 2 * 128, "masks")
        for i in range(2):
            dma("poolq", MASKS[0].ap(0, 128, i * 128, (i + 1) * 128),
                masks_in[i, :, :], (), ("masks",))
        o_bf = [AL.buf(BF16, S, f"obf{t}") for t in range(4)]
        sim_f = [AL.buf(F32, 640, f"simf{i}") for i in range(4)]
        exf = [AL.buf(F32, 640, f"exf{i}") for i in range(4)]
        at_tp = [AL.buf(BF16, 640, f"attp{i}") for i in range(4)]
        att_t = [AL.buf(BF16, 640, f"att_t{i}") for i in range(4)]
        smax = AL.buf(F32, 16, "smax")
        ssum = AL.buf(F32, 16, "ssum")
        attention(l, q_bf, k_bf, v_tm, o_bf, sim_f, exf, at_tp, att_t, smax, ssum)

        wqs2 = [AL.buf(BF16, 512, f"wos{i}") for i in range(2)]
        wo_r = wof[l].rearrange("(k p) n -> p k n", p=128)
        for m in range(NT):
            ws = wqs2[m % 2]
            dma("poolq", ws.ap().rearrange("p (k n) -> p k n", n=128),
                wo_r[:, :, m * 128:(m + 1) * 128], (), (ws.key,))
            mm_kmajor(
                4,
                lambda k, ws=ws: ws.ap(0, 128, k * 128, (k + 1) * 128),
                lambda k, c: o_bf[k].ap(0, 128, c * CK, (c + 1) * CK),
                [0, 1, 2, 3],
                lambda k, ws=ws: (ws.key,) + tuple(o.key for o in o_bf),
                lambda c, bk, m=m: vop(lambda eng, m=m, c=c, bk=bk: eng.tensor_tensor(
                    x_cm[m].ap(0, 128, 1 + c * CK, 1 + (c + 1) * CK),
                    x_cm[m].ap(0, 128, 1 + c * CK, 1 + (c + 1) * CK),
                    bank(bk), OP.add),
                    reads=(f"ps{bk}", f"x{m}"), writes=(f"x{m}",)))
        AL.release()
        AL.release()
        P.epoch()

        AL.mark()
        hb = [AL.buf(BF16, S, f"h2b{t}") for t in range(NT)]
        fga = AL.buf(F32, CK, "lnfga")
        sqt = [AL.buf(F32, CK, "sqf0"), AL.buf(F32, CK, "sqf1")]
        for t in range(NT):
            so = 1 if t < 4 else 0
            vop(lambda eng, t=t, so=so: eng.copy(hb[t].ap(),
                                                 x_cm[t].ap(0, 128, so, so + S)),
                reads=(f"x{t}",), writes=(hb[t].key,), eng="act")
        for c in range(NC4):
            stats_chunk(c, None,
                        lambda t, c=c: x_cm[t].ap(0, 128, (1 if t < 4 else 0) + c * CK,
                                                  (1 if t < 4 else 0) + (c + 1) * CK),
                        sqt)
            bcast_rows()
            for t in range(NT):
                so = 1 if t < 4 else 0
                vop(lambda eng, t=t, c=c, so=so: eng.tensor_tensor(
                    fga.ap(), x_cm[t].ap(0, 128, so + c * CK, so + (c + 1) * CK),
                    bank(5), OP.mult),
                    reads=(f"x{t}", "ps5"), writes=("lnfga",))
                vop(lambda eng: eng.tensor_tensor(
                    fga.ap(), fga.ap(), bank(6), OP.subtract),
                    reads=("lnfga", "ps6"), writes=("lnfga",))
                vop(lambda eng, t=t, c=c, l=l: eng.tensor_scalar(
                    hb[t].ap(0, 128, c * CK, (c + 1) * CK), fga.ap(),
                    g_pp[l].ap(0, 128, t, t + 1), bb_pp[l].ap(0, 128, t, t + 1),
                    OP.mult, OP.add),
                    reads=("lnfga", g_pp[l].key, bb_pp[l].key), writes=(hb[t].key,))
        h3 = [AL.buf(BF16, 1024, f"h3_{j}") for j in range(NJ)]
        w1s = [AL.buf(BF16, 2 * CH, f"w1s{i}") for i in range(3)]
        w2sl = [AL.buf(BF16, 512, f"w2sl{i}") for i in range(4)]
        gel = [AL.buf(BF16, CK, f"gel{i}") for i in range(2)]
        w1r = w1[l].rearrange("(k p) n -> p k n", p=128)
        for cc in range(2):
            c0 = cc * 1024
            vop(lambda eng: eng.memset(h3[21].ap(), 0.0), writes=(h3[21].key,))
            for j in range(NJ):
                ws1 = w1s[j % 3]
                M = 128 if j < 21 else 42
                dma("poolq",
                    ws1.ap(0, 128, 0, 1024).rearrange("p (k n) -> p k n", n=128)[:, :, 0:M],
                    w1r[:, :, j * 128:j * 128 + M], (), (ws1.key,))
                dma("poolq",
                    ws1.ap(0, 128, 1024, 2048).rearrange("p (k n) -> p k n", n=128)[:, :, 0:M],
                    w1r[:, :, HID + j * 128:HID + j * 128 + M], (), (ws1.key,))
                ba = [0, 1] if j % 2 == 0 else [4, 5]
                bg = [2, 3] if j % 2 == 0 else [6, 7]
                mm_kmajor(
                    8,
                    lambda k, ws1=ws1, M=M: ws1.ap(0, 128, k * 128, k * 128 + M),
                    lambda k, sub, c0=c0: hb[k].ap(0, 128, c0 + sub * CK,
                                                   c0 + (sub + 1) * CK),
                    ba,
                    lambda k, ws1=ws1: (ws1.key,) + tuple(h.key for h in hb),
                    lambda sub, bk: None, pslice=(0, 128 if j < 21 else 42))
                mm_kmajor(
                    8,
                    lambda k, ws1=ws1, M=M: ws1.ap(0, 128, 1024 + k * 128,
                                                   1024 + k * 128 + M),
                    lambda k, sub, c0=c0: hb[k].ap(0, 128, c0 + sub * CK,
                                                   c0 + (sub + 1) * CK),
                    bg,
                    lambda k, ws1=ws1: (ws1.key,) + tuple(h.key for h in hb),
                    lambda sub, bk: None, pslice=(0, 128 if j < 21 else 42))
                for sub in range(2):
                    gl = gel[sub]
                    vop(lambda eng, M=M, bkg=bg[sub], j=j, l=l, gl=gl: eng.activation(
                        gl.ap(0, M, 0, CK), bank(bg[sub] if False else bkg, 0, M),
                        AF.Gelu, bias=b1g_pp[l].ap(0, M, j, j + 1)),
                        reads=(f"ps{bg[sub]}", b1g_pp[l].key), writes=(gl.key,),
                        eng="act")
                    vop(lambda eng, M=M, bka=ba[sub], j=j, l=l, sub=sub, gl=gl:
                        eng.scalar_tensor_tensor(
                            h3[j].ap(0, M, sub * CK, (sub + 1) * CK),
                            bank(bka, 0, M), b1a_pp[l].ap(0, M, j, j + 1),
                            gl.ap(0, M, 0, CK), OP.add, OP.mult),
                        reads=(f"ps{ba[sub]}", b1a_pp[l].key, gl.key),
                        writes=(h3[j].key,))
            for half in range(2):
                w2g = [Group(f"ps{bi}", inc_each=True) for bi in range(8)]
                for j in range(NJ):
                    ws2 = w2sl[j % 4]
                    Mk = 128 if j < 21 else 42
                    if j == 21:
                        vop(lambda eng, ws2=ws2: eng.memset(ws2.ap(), 0.0),
                            writes=(ws2.key,))
                    dma("poolq", ws2.ap(0, Mk, 0, 512),
                        w2[l, j * 128:j * 128 + Mk, half * 512:(half + 1) * 512],
                        (), (ws2.key,))
                    for hti in range(4):
                        for sub in range(2):
                            bi = hti * 2 + sub
                            w2g[bi].member(
                                bank(bi),
                                ws2.ap(0, 128, hti * 128, (hti + 1) * 128),
                                h3[j].ap(0, 128, sub * CK, (sub + 1) * CK),
                                (ws2.key, h3[j].key), last=(j == NJ - 1))
                for hti in range(4):
                    ht = half * 4 + hti
                    for sub in range(2):
                        bi = hti * 2 + sub
                        vop(lambda eng, ht=ht, c0=c0, sub=sub, bi=bi, l=l:
                            eng.scalar_tensor_tensor(
                                x_cm[ht].ap(0, 128, 1 + c0 + sub * CK,
                                            1 + c0 + (sub + 1) * CK), bank(bi),
                                b2_pp[l].ap(0, 128, ht, ht + 1),
                                x_cm[ht].ap(0, 128, 1 + c0 + sub * CK,
                                            1 + c0 + (sub + 1) * CK),
                                OP.add, OP.add),
                            reads=(f"ps{bi}", b2_pp[l].key, f"x{ht}"),
                            writes=(f"x{ht}",))
        AL.release()
        P.epoch()

    def final_ln_out(b):
        AL.mark()
        ld = [AL.buf(F32, S, f"rb{i}") for i in range(2)]
        for t in range(NT):
            L = ld[t % 2]
            dma("syncq", L.ap(), x1sp[t, :, :], ("x1sp",), (L.key,))
            vop(lambda eng, t=t, L=L: eng.tensor_tensor(
                x_cm[t].ap(0, 128, 1, S + 1), x_cm[t].ap(0, 128, 1, S + 1),
                L.ap(), OP.add),
                reads=(f"x{t}", L.key), writes=(f"x{t}",))
        AL.release()
        AL.mark()
        sqt = [AL.buf(F32, CK, "sqf0"), AL.buf(F32, CK, "sqf1")]
        for c in range(NC4):
            stats_chunk(c, None,
                        lambda t, c=c: x_cm[t].ap(0, 128, 1 + c * CK, 1 + (c + 1) * CK),
                        sqt)
            bcast_rows()
            for t in range(NT):
                vop(lambda eng, t=t, c=c: eng.tensor_tensor(
                    x_cm[t].ap(0, 128, 1 + c * CK, 1 + (c + 1) * CK),
                    x_cm[t].ap(0, 128, 1 + c * CK, 1 + (c + 1) * CK),
                    bank(5), OP.mult),
                    reads=(f"x{t}", "ps5"), writes=(f"x{t}",))
                vop(lambda eng, t=t, c=c: eng.tensor_tensor(
                    x_cm[t].ap(0, 128, 1 + c * CK, 1 + (c + 1) * CK),
                    x_cm[t].ap(0, 128, 1 + c * CK, 1 + (c + 1) * CK),
                    bank(6), OP.subtract),
                    reads=(f"x{t}", "ps6"), writes=(f"x{t}",))
        for t in range(NT):
            dma("syncq", y_out[b, t * 128:(t + 1) * 128, :],
                x_cm[t].ap(0, 128, 1, S + 1), (f"x{t}",), (f"y{b}_{t}",))
        AL.release()
        P.epoch()

    for b in range(NB):
        load_x(b)
        ssm(b)
        for t in range(NT):
            dma("syncq", x1sp[t, :, :], x_cm[t].ap(0, 128, 1, S + 1),
                (f"x{t}",), ("x1sp",))
        for l in range(2):
            ffn_layer(b, l)
        final_ln_out(b)

    fin = {a: P.count[a] for a in ACCOUNTS if P.count[a]}
    P.ops["syncq"].append((fin, lambda eng: None, 0, "sq0"))

    with nc.Block() as block:
        @block.tensor
        def _(eng):
            P.replay("te", eng, sems)

        @block.vector
        def _(eng):
            P.replay("dve", eng, sems)

        @block.scalar
        def _(eng):
            P.replay("act", eng, sems)

        @block.sync
        def _(eng):
            P.replay("syncq", eng, sems)

        @block.gpsimd
        def _(eng):
            P.replay("poolq", eng, sems)

    ctx.close()
    return nc


_NC_CACHE = [None]


def kernel(L0_attn_x, L0_attn_Wf, L0_attn_Wi, L0_attn_Wv, L0_attn_Wo,
           L0_ffn_wqkv, L0_ffn_q_scale, L0_ffn_k_scale, L0_ffn_wo,
           L0_ffn_ln_g, L0_ffn_ln_b, L0_ffn_w1, L0_ffn_b1, L0_ffn_w2,
           L0_ffn_b2):
    x = np.ascontiguousarray(np.asarray(L0_attn_x, dtype=np.float32))
    cast = lambda a: np.ascontiguousarray(np.asarray(a, dtype=np.float32))
    tables = _mk_tables()
    e2h = np.zeros((128, 2), np.float32)
    e2h[0:64, 0] = 1.0
    e2h[64:128, 1] = 1.0
    ebh = np.zeros((2, 128), np.float32)
    ebh[0, 0:64] = 1.0
    ebh[1, 64:128] = 1.0
    castT = lambda a: np.ascontiguousarray(np.asarray(a, dtype=np.float32).T)
    common = dict(
        Wf=castT(L0_attn_Wf), Wi=castT(L0_attn_Wi), Wv=castT(L0_attn_Wv),
        Wo=castT(L0_attn_Wo), wqkv=cast(L0_ffn_wqkv), wof=cast(L0_ffn_wo),
        w1=cast(L0_ffn_w1), w2=cast(L0_ffn_w2), qsc=cast(L0_ffn_q_scale),
        ksc=cast(L0_ffn_k_scale), lng=cast(L0_ffn_ln_g), lnb=cast(L0_ffn_ln_b),
        b1=cast(L0_ffn_b1), b2=cast(L0_ffn_b2),
        masks=_mk_masks(),
        idf=np.eye(128, dtype=np.float32),
        idb=np.eye(128, dtype=np.float32).astype(BFD),
        prot=_mk_prot().astype(BFD),
        onec=np.ones((128, 1), np.float32).astype(BFD),
        oner=np.ones((1, 128), np.float32),
        e2=e2h.astype(BFD), e2f=e2h, eb=ebh,
        onecf=np.ones((128, 1), np.float32),
        **tables,
    )
    if _NC_CACHE[0] is None:
        _NC_CACHE[0] = build_program()
    nc = _NC_CACHE[0]
    in_maps = []
    for core in range(8):
        m = dict(common)
        m["x"] = np.ascontiguousarray(x[2 * core:2 * core + 2].transpose(0, 2, 1))
        in_maps.append(m)
    res = run_bass_kernel_spmd(nc, in_maps, core_ids=list(range(8)))
    return np.ascontiguousarray(
        np.concatenate([r["y"] for r in res.results], axis=0).transpose(0, 2, 1))


# version 17 (final)
# speedup vs baseline: 1.0683x; 1.0261x over previous
from contextlib import ExitStack

import numpy as np
import ml_dtypes

import concourse.bass as bass
import concourse.mybir as mybir
from concourse.bass_utils import run_bass_kernel_spmd

F32 = mybir.dt.float32
BF16 = mybir.dt.bfloat16
AX = mybir.AxisListType.X
AF = mybir.ActivationFunctionType
OP = mybir.AluOpType
BFD = ml_dtypes.bfloat16

S = 2048
CH = 1024
NT = 8
NB = 2
CK = 512
NC4 = 4
HID = 2730
NJ = 22
DH = 64
WIN = 512
NEG = np.float32(-3.0e38)

STREAMS = ("te", "dve", "act", "syncq", "poolq")
NQC = 4
ACCOUNTS = ("te", "dve", "act", "plc") + tuple(f"sq{i}" for i in range(NQC)) \
    + tuple(f"pq{i}" for i in range(NQC))
QCLS = {"syncq": "sq", "poolq": "pq"}

F32_COLS = 27360
BF_COLS = 51680


class Plan:

    def __init__(self):
        self.ops = {e: [] for e in STREAMS}
        self.count = {a: 0 for a in ACCOUNTS}
        self.qidx = {q: 0 for q in QCLS}
        self.last_write = {}
        self.readers = {}
        self.watermark = {a: {} for a in ACCOUNTS}
        self.epoch_marks = {a: 0 for a in ACCOUNTS}

    def waits_for(self, stream, account, reads, writes):
        waits = {}

        def need(src, cnt):
            if cnt <= 0:
                return
            if src == account and src not in ("te", "dve", "act"):
                return
            if cnt > self.watermark[account].get(src, 0) and \
                    cnt > waits.get(src, 0):
                waits[src] = cnt

        for k in reads:
            w = self.last_write.get(k)
            if w:
                need(*w)
        for k in writes:
            w = self.last_write.get(k)
            if w:
                need(*w)
            for src, cnt in self.readers.get(k, {}).items():
                need(src, cnt)
        if writes:
            for src, cnt in self.epoch_marks.items():
                need(src, cnt)
        return waits

    def emit(self, stream, account, fn, waits, inc):
        for src, cnt in waits.items():
            if cnt > self.watermark[account].get(src, 0):
                self.watermark[account][src] = cnt
        self.count[account] += inc
        self.ops[stream].append((waits, fn, inc, account))
        return self.count[account]

    def record(self, account, cnt, reads, writes):
        for k in reads:
            self.readers.setdefault(k, {})[account] = cnt
        for k in writes:
            self.last_write[k] = (account, cnt)
            self.readers[k] = {}

    def op(self, stream, fn, reads=(), writes=(), inc=1):
        if stream in QCLS:
            i = self.qidx[stream]
            self.qidx[stream] = i + 1
            account = f"{QCLS[stream]}{i % NQC}"
            waits = self.waits_for(stream, account, reads, writes)
            prior = 16 * (i // NQC)
            if prior > 0:
                waits[account] = prior
            cnt = self.emit(stream, account, fn, waits, 16)
            self.record(account, cnt, reads, writes)
        else:
            account = stream
            waits = self.waits_for(stream, account, reads, writes)
            cnt = self.emit(stream, account, fn, waits, inc)
            self.record(account, cnt, reads, writes)

    def epoch(self):
        self.epoch_marks = dict(self.count)

    def replay(self, name, eng, sems):
        for waits, fn, inc, account in self.ops[name]:
            for src in sorted(waits):
                eng.wait_ge(sems[src], waits[src])
            ins = fn(eng)
            if inc and ins is not None:
                ins.then_inc(sems[account], inc)


class Region:
    def __init__(self, tensor, off, cols, name):
        self.tensor = tensor
        self.off = off
        self.cols = cols
        self.key = name

    def ap(self, p0=0, p1=128, c0=0, c1=None):
        c1 = self.cols if c1 is None else c1
        assert 0 <= c0 <= c1 <= self.cols, (self.key, c0, c1, self.cols)
        return self.tensor[p0:p1, self.off + c0:self.off + c1]


class Alloc:
    def __init__(self, tensors, sizes):
        self.t = tensors
        self.size = sizes
        self.off = {k: 0 for k in tensors}
        self.marks = []
        self.peak = {k: 0 for k in tensors}
        self.on_release = None

    def buf(self, dtype, cols, name):
        o = (self.off[dtype] + 15) // 16 * 16
        self.off[dtype] = o + cols
        self.peak[dtype] = max(self.peak[dtype], self.off[dtype])
        assert self.off[dtype] <= self.size[dtype], (name, dtype, self.off[dtype])
        return Region(self.t[dtype], o, cols, name)

    def mark(self):
        self.marks.append(dict(self.off))

    def release(self):
        self.off = self.marks.pop()
        if self.on_release is not None:
            self.on_release()


def _mk_tables():
    import jax
    import jax.numpy as jnp
    try:
        cpu = jax.devices("cpu")[0]
    except RuntimeError:
        cpu = None
    from contextlib import nullcontext
    with (jax.default_device(cpu) if cpu is not None else nullcontext()):
        half32 = jnp.arange(0, DH, 2, dtype=jnp.float32)
        inv_freq32 = 1.0 / 10000.0 ** (half32 / DH)
        t32 = jnp.arange(S, dtype=jnp.float32)
        freqs32 = t32[:, None] * inv_freq32[None, :]
        freqs32 = jnp.concatenate([freqs32, freqs32], axis=-1)
        base32 = (half32 + 0.4 * DH) / (1.4 * DH)
        base32 = jnp.concatenate([base32, base32])
        power32 = (t32 - S // 2) / float(WIN // 2)
        scale32 = base32[None, :] ** power32[:, None]
        cos32, sin32 = jnp.cos(freqs32), jnp.sin(freqs32)
    cos = np.asarray(cos32, np.float64)
    sin = np.asarray(sin32, np.float64)
    scale = np.asarray(scale32, np.float64)
    sign = np.ones(DH)
    sign[DH // 2:] = -1.0
    out = {}
    for nm, tab in (("csq", 8.0 * cos * scale), ("ssq", 8.0 * sin * scale * sign),
                    ("csk", cos / scale), ("ssk", sin / scale * sign)):
        tt = np.ascontiguousarray(tab.T)
        tt = np.concatenate([tt, tt], axis=0)
        hi = tt.astype(np.float32)
        lo = (tt - hi.astype(np.float64)).astype(np.float32)
        out[nm + "_hi"] = np.ascontiguousarray(hi)
        out[nm + "_lo"] = np.ascontiguousarray(lo)
    return out


def _mk_masks():
    m = np.zeros((2, 128, 128), np.float32)
    i = np.arange(128)[:, None]
    j = np.arange(128)[None, :]
    m[0] = np.where(j >= i, 0.0, NEG)
    m[1] = np.where(j <= i, 0.0, NEG)
    return m


def _mk_prot():
    p = np.zeros((128, 128), np.float32)
    for mcol in range(128):
        partner = mcol + 32 if (mcol % 64) < 32 else mcol - 32
        p[partner, mcol] = 1.0
    return p


def build_program():
    nc = bass.Bass("TRN2", target_bir_lowering=False)

    def din(name, shape, dt=F32):
        return nc.dram_tensor(name, shape, dt, kind="ExternalInput").ap()

    x_in = din("x", [NB, CH, S])
    Wg = [din(n, [CH, CH]) for n in ("Wf", "Wi", "Wv")]
    Wo = din("Wo", [CH, CH])
    wqkv = din("wqkv", [2, CH, 1536])
    wof = din("wof", [2, 512, CH])
    w1 = din("w1", [2, CH, 2 * HID])
    w2 = din("w2", [2, HID, CH])
    qsc = din("qsc", [2, DH])
    ksc = din("ksc", [2, DH])
    lng = din("lng", [2, CH])
    lnb = din("lnb", [2, CH])
    b1 = din("b1", [2, 2 * HID])
    b2 = din("b2", [2, CH])
    masks_in = din("masks", [2, 128, 128])
    tabs_in = {nm: din(nm, [128, S]) for nm in
               ("csq_hi", "csq_lo", "ssq_hi", "ssq_lo",
                "csk_hi", "csk_lo", "ssk_hi", "ssk_lo")}
    idf_in = din("idf", [128, 128])
    idb_in = din("idb", [128, 128], BF16)
    prot_in = din("prot", [128, 128], BF16)
    onec_in = din("onec", [128, 1], BF16)
    oner_in = din("oner", [1, 128])
    e2_in = din("e2", [128, 2], BF16)
    eb_in = din("eb", [2, 128])

    y_out = nc.dram_tensor("y", [NB, CH, S], F32, kind="ExternalOutput").ap()
    x1sp = nc.dram_tensor("x1sp", [NT, 128, S], F32, kind="Internal").ap()

    P = Plan()
    ctx = ExitStack()
    a32 = ctx.enter_context(nc.sbuf_tensor("a32", [128, F32_COLS], F32))
    a16 = ctx.enter_context(nc.sbuf_tensor("a16", [128, BF_COLS], BF16))
    AL = Alloc({F32: a32, BF16: a16}, {F32: F32_COLS, BF16: BF_COLS})
    AL.on_release = lambda: P.epoch()
    psum = ctx.enter_context(nc.psum_tensor("ps", [128, 4096], F32))
    sems = {a: ctx.enter_context(nc.semaphore("sem_" + a)) for a in ACCOUNTS}

    def bank(i, p0=0, p1=128, c0=0, c1=512):
        assert 0 <= c0 <= c1 <= 512
        return psum[p0:p1, i * 512 + c0:i * 512 + c1]

    ident_f = AL.buf(F32, 128, "identf")
    oner = AL.buf(F32, 128, "oner")
    eb = AL.buf(F32, 128, "eb")
    rowM = AL.buf(F32, CK, "rowM")
    rowV = AL.buf(F32, CK, "rowV")
    rowI = AL.buf(F32, CK, "rowI")
    rowR = AL.buf(F32, CK, "rowR")
    rowT = AL.buf(F32, S, "rowT")
    rql = AL.buf(F32, 32, "rql")
    x_cm = [AL.buf(F32, S + 1, f"x{t}") for t in range(NT)]
    ident_b = AL.buf(BF16, 128, "identb")
    prot = AL.buf(BF16, 128, "prot")
    onec = AL.buf(BF16, 16, "onec")
    e2 = AL.buf(BF16, 16, "e2")
    prm = AL.buf(F32, 144, "prm")
    def _sub(l, off, cols):
        return Region(prm.tensor, prm.off + l * 72 + off, cols, "prm")
    g_pp = [_sub(l, 0, 8) for l in range(2)]
    bb_pp = [_sub(l, 8, 8) for l in range(2)]
    qs_pp = [_sub(l, 16, 1) for l in range(2)]
    ks_pp = [_sub(l, 17, 1) for l in range(2)]
    b1a_pp = [_sub(l, 18, NJ) for l in range(2)]
    b1g_pp = [_sub(l, 40, NJ) for l in range(2)]
    b2_pp = [_sub(l, 62, 8) for l in range(2)]
    CST = [None]
    TABS = [None]
    MASKS = [None]

    def dma(q, out_ap, in_ap, reads=(), writes=(), slow=False):
        P.op(q, (lambda eng, o=out_ap, i=in_ap, s=slow: eng.dma_start(
            out=o, in_=i, allow_slow_non_contiguous=s)),
             reads=reads, writes=writes)

    class Group:

        def __init__(self, bank_key, inc_each=False):
            self.bank_key = bank_key
            self.inc_each = inc_each
            self.first = True

        def member(self, o, lt, r, rd, last=False):
            w = (self.bank_key,) if self.first else ()
            waits = P.waits_for("te", "te", rd, w)
            st = self.first
            self.first = False
            sp = last
            inc = 1 if (self.inc_each or last) else 0
            cnt = P.emit("te", "te", (lambda eng, o=o, lt=lt, r=r, st=st, sp=sp:
                                      (eng.matmul(o, lt, r, start=st, stop=sp),
                                       None)[1] if not (sp or True) else
                                      eng.matmul(o, lt, r, start=st, stop=sp)),
                         waits, inc)
            P.record("te", cnt if inc else P.count["te"] + 1, rd,
                     (self.bank_key,) if last else ())

    def mm_members(members, bank_key):
        g = Group(bank_key)
        for i, (o, lt, r, rd) in enumerate(members):
            g.member(o, lt, r, rd, last=(i == len(members) - 1))

    def mmg(items, reads, writes):
        mm_members([(o, lt, r, reads if i == 0 else ())
                    for i, (o, lt, r) in enumerate(items)], writes[0])

    def mm_kmajor(nk, lhsT_of_k, rhs_of_kc, banks, reads_of_k, evac,
                  pslice=(0, 128)):
        first = [True] * len(banks)
        p0, p1 = pslice
        for k in range(nk):
            for ci, bk in enumerate(banks):
                rd = reads_of_k(k) if (ci == 0 or k == nk - 1) else ()
                w = (f"ps{bk}",) if first[ci] else ()
                waits = P.waits_for("te", "te", rd, w)
                st = first[ci]
                first[ci] = False
                sp = (k == nk - 1)
                inc = 1 if sp else 0
                lt = lhsT_of_k(k)
                r = rhs_of_kc(k, ci)
                cnt = P.emit("te", "te",
                             (lambda eng, lt=lt, r=r, st=st, sp=sp, bk=bk:
                              eng.matmul(bank(bk, p0, p1), lt, r, start=st, stop=sp)),
                             waits, inc)
                P.record("te", cnt if inc else P.count["te"] + 1, rd,
                         (f"ps{bk}",) if sp else ())
        for ci, bk in enumerate(banks):
            if evac is not None:
                evac(ci, bk)

    def mm1(out_ap, lt_ap, r_ap, reads, bank_key):
        P.op("te", (lambda eng, o=out_ap, lt=lt_ap, r=r_ap:
                    eng.matmul(o, lt, r, start=True, stop=True)),
             reads=reads, writes=(bank_key,), inc=1)

    def vop(fn, reads=(), writes=(), eng="dve"):
        if eng == "poolc":
            waits = P.waits_for("poolq", "plc", reads, writes)
            cnt = P.emit("poolq", "plc", fn, waits, 1)
            P.record("plc", cnt, reads, writes)
        else:
            P.op(eng, fn, reads=reads, writes=writes, inc=1)

    dma("syncq", ident_f.ap(), idf_in[:, :], (), ("identf",))
    dma("syncq", oner.ap(0, 1, 0, 128), oner_in[:, :], (), ("oner",))
    dma("syncq", eb.ap(0, 2, 0, 128), eb_in[:, :], (), ("eb",))
    dma("syncq", eb.ap(32, 34, 0, 128), eb_in[:, :], (), ("eb",))
    dma("syncq", ident_b.ap(), idb_in[:, :], (), ("identb",))
    dma("syncq", prot.ap(), prot_in[:, :], (), ("prot",))
    dma("syncq", onec.ap(0, 128, 0, 1), onec_in[:, :], (), ("onec",))
    dma("syncq", e2.ap(0, 128, 0, 2), e2_in[:, :], (), ("e2",))
    for l in range(2):
        dma("syncq", g_pp[l].ap(), lng[l, :].rearrange("(t p) -> p t", p=128),
            (), (g_pp[l].key,), slow=True)
        dma("syncq", bb_pp[l].ap(), lnb[l, :].rearrange("(t p) -> p t", p=128),
            (), (bb_pp[l].key,), slow=True)
        dma("syncq", qs_pp[l].ap(0, 64), qsc[l:l+1, :].transpose([1, 0]),
            (), (qs_pp[l].key,), slow=True)
        dma("syncq", qs_pp[l].ap(64, 128), qsc[l:l+1, :].transpose([1, 0]),
            (), (qs_pp[l].key,), slow=True)
        dma("syncq", ks_pp[l].ap(0, 64), ksc[l:l+1, :].transpose([1, 0]),
            (), (ks_pp[l].key,), slow=True)
        dma("syncq", ks_pp[l].ap(64, 128), ksc[l:l+1, :].transpose([1, 0]),
            (), (ks_pp[l].key,), slow=True)
        dma("syncq", b1a_pp[l].ap(0, 128, 0, 21),
            b1[l, 0:2688].rearrange("(j p) -> p j", p=128), (), (b1a_pp[l].key,), slow=True)
        dma("syncq", b1a_pp[l].ap(0, 42, 21, 22),
            b1[l:l+1, 2688:2730].transpose([1, 0]), (), (b1a_pp[l].key,), slow=True)
        dma("syncq", b1g_pp[l].ap(0, 128, 0, 21),
            b1[l, HID:HID + 2688].rearrange("(j p) -> p j", p=128), (), (b1g_pp[l].key,), slow=True)
        dma("syncq", b1g_pp[l].ap(0, 42, 21, 22),
            b1[l:l+1, HID + 2688:2 * HID].transpose([1, 0]), (), (b1g_pp[l].key,), slow=True)
        dma("syncq", b2_pp[l].ap(), b2[l, :].rearrange("(t p) -> p t", p=128),
            (), (b2_pp[l].key,), slow=True)
    for t in range(NT):
        vop(lambda eng, t=t: eng.memset(x_cm[t].ap(0, 128, 0, 1), 0.0),
            writes=(x_cm[t].key,))
    P.epoch()

    bmi = [0]

    def nbm():
        b = bmi[0] % 4
        bmi[0] += 1
        return b

    def stats_chunk(c, sum_tiles, xsrc):
        if sum_tiles is not None:
            mems = [(bank(7, 0, 1, 0, 512), onec.ap(0, 128, 0, 1),
                     sum_tiles[t].ap(0, 128, c * CK, (c + 1) * CK),
                     (sum_tiles[t].key, "onec")) for t in range(NT)]
            mm_members(mems, "ps7")
        else:
            g = Group("ps7", inc_each=True)
            for t in range(NT):
                cb = CST[0][t % 2]
                vop(lambda eng, t=t, cb=cb: eng.copy(cb.ap(), xsrc(t)),
                    reads=(f"x{t}",), writes=(cb.key,), eng="act")
                g.member(bank(7, 0, 1, 0, 512), onec.ap(0, 128, 0, 1),
                         cb.ap(), (cb.key, "onec"), last=(t == NT - 1))
        vop(lambda eng: eng.tensor_scalar(rowM.ap(0, 1), bank(7, 0, 1, 0, 512),
                                          1.0 / CH, None, OP.mult),
            reads=("ps7",), writes=("rowM",))
        g = Group("ps7", inc_each=True)
        for t in range(NT):
            cb = CST[0][t % 2]
            vop(lambda eng, t=t, cb=cb: eng.activation(cb.ap(), xsrc(t), AF.Square),
                reads=(f"x{t}",), writes=(cb.key,), eng="act")
            g.member(bank(7, 32, 33, 0, 512), onec.ap(0, 128, 0, 1),
                     cb.ap(), (cb.key, "onec"), last=(t == NT - 1))
        vop(lambda eng: eng.activation(rowV.ap(0, 1), rowM.ap(0, 1), AF.Square),
            reads=("rowM",), writes=("rowV",), eng="act")
        vop(lambda eng: eng.tensor_scalar(rowV.ap(0, 1), rowV.ap(0, 1),
                                          1e-5, None, OP.subtract),
            reads=("rowV",), writes=("rowV",))
        vop(lambda eng: eng.scalar_tensor_tensor(
            rowV.ap(0, 1), bank(7, 32, 33, 0, 512), 1.0 / CH, rowV.ap(0, 1),
            OP.mult, OP.subtract),
            reads=("ps7", "rowV"), writes=("rowV",))
        vop(lambda eng: eng.activation(rowV.ap(0, 1), rowV.ap(0, 1), AF.Sqrt),
            reads=("rowV",), writes=("rowV",), eng="act")
        vop(lambda eng: eng.reciprocal(rowI.ap(0, 1), rowV.ap(0, 1)),
            reads=("rowV",), writes=("rowI",))
        vop(lambda eng: eng.tensor_tensor(rowR.ap(0, 1), rowM.ap(0, 1),
                                          rowI.ap(0, 1), OP.mult),
            reads=("rowM", "rowI"), writes=("rowR",))

    def bcast_rows():
        mm1(bank(5), oner.ap(0, 1, 0, 128), rowI.ap(0, 1, 0, 512),
            ("rowI", "oner"), "ps5")
        mm1(bank(6), oner.ap(0, 1, 0, 128), rowR.ap(0, 1, 0, 512),
            ("rowR", "oner"), "ps6")

    def outer_ln(sum_tiles):
        for c in range(NC4):
            stats_chunk(c, sum_tiles,
                        lambda t, c=c: x_cm[t].ap(0, 128, 1 + c * CK, 1 + (c + 1) * CK))
            bcast_rows()
            for t in range(NT):
                vop(lambda eng, t=t, c=c: eng.tensor_tensor(
                    x_cm[t].ap(0, 128, 1 + c * CK, 1 + (c + 1) * CK),
                    x_cm[t].ap(0, 128, 1 + c * CK, 1 + (c + 1) * CK),
                    bank(5), OP.mult),
                    reads=(f"x{t}", "ps5"), writes=(f"x{t}",))
                vop(lambda eng, t=t, c=c: eng.tensor_tensor(
                    x_cm[t].ap(0, 128, 1 + c * CK, 1 + (c + 1) * CK),
                    x_cm[t].ap(0, 128, 1 + c * CK, 1 + (c + 1) * CK),
                    bank(6), OP.subtract),
                    reads=(f"x{t}", "ps6"), writes=(f"x{t}",))

    def load_x(b):
        for t in range(NT):
            dma("syncq", x_cm[t].ap(0, 128, 1, S + 1),
                x_in[b, t * 128:(t + 1) * 128, :], (), (f"x{t}",))
        P.epoch()

    def ssm(b):
        AL.mark()
        x_bf = [AL.buf(BF16, S, f"xb{t}") for t in range(NT)]
        s_bf = [AL.buf(BF16, S, f"sb{t}") for t in range(NT)]
        fga = AL.buf(F32, S, "fga")
        fgb = AL.buf(F32, S, "fgb")
        fgc = AL.buf(F32, S, "fgc")
        wsl = [AL.buf(BF16, 3 * CH, f"wsl{i}") for i in range(2)]
        CST[0] = [AL.buf(BF16, CK, "cst0"), AL.buf(BF16, CK, "cst1")]
        for t in range(NT):
            vop(lambda eng, t=t: eng.copy(x_bf[t].ap(), x_cm[t].ap(0, 128, 1, S + 1)),
                reads=(f"x{t}",), writes=(x_bf[t].key,), eng="act")
        actf = (AF.Tanh, AF.Sigmoid, AF.Silu)
        outs = (fga, fgb, fgc)
        WgT = [Wg[gi].rearrange("(k p) n -> p k n", p=128) for gi in range(3)]
        WoT = Wo.rearrange("(k p) n -> p k n", p=128)
        for nt in range(NT):
            tr = wsl[nt % 2]
            for gi in range(3):
                dma("poolq", tr.ap(0, 128, gi * CH, (gi + 1) * CH).rearrange(
                    "p (k n) -> p k n", n=128),
                    WgT[gi][:, :, nt * 128:(nt + 1) * 128], (), (tr.key,))
            for gi in range(3):
                mm_kmajor(
                    8,
                    lambda k, gi=gi, tr=tr: tr.ap(0, 128, gi * CH + k * 128,
                                                  gi * CH + (k + 1) * 128),
                    lambda k, c: x_bf[k].ap(0, 128, c * CK, (c + 1) * CK),
                    [0, 1, 2, 3] if gi % 2 == 0 else [4, 5, 6, 7],
                    lambda k, tr=tr: (tr.key,) + tuple(x.key for x in x_bf),
                    lambda c, bk, gi=gi: vop(lambda eng, gi=gi, c=c, bk=bk: eng.activation(
                        outs[gi].ap(0, 128, c * CK, (c + 1) * CK), bank(bk), actf[gi]),
                        reads=(f"ps{bk}",), writes=(outs[gi].key,), eng="act"))
            vop(lambda eng: eng.tensor_tensor(fgb.ap(), fgb.ap(), fgc.ap(), OP.mult),
                reads=("fgb", "fgc"), writes=("fgb",))
            vop(lambda eng: eng.tensor_tensor_scan(fgc.ap(), fga.ap(), fgb.ap(),
                                                   0.0, OP.mult, OP.add),
                reads=("fga", "fgb"), writes=("fgc",))
            vop(lambda eng, nt=nt: eng.activation(s_bf[nt].ap(), fgc.ap(), AF.Silu),
                reads=("fgc",), writes=(s_bf[nt].key,), eng="act")
        for ht in range(NT):
            tr = wsl[ht % 2]
            dma("poolq", tr.ap(0, 128, 0, CH).rearrange("p (k n) -> p k n", n=128),
                WoT[:, :, ht * 128:(ht + 1) * 128], (), (tr.key,))
            mm_kmajor(
                8,
                lambda k, tr=tr: tr.ap(0, 128, k * 128, (k + 1) * 128),
                lambda k, c: s_bf[k].ap(0, 128, c * CK, (c + 1) * CK),
                [0, 1, 2, 3] if ht % 2 == 0 else [4, 5, 6, 7],
                lambda k, tr=tr: (tr.key,) + tuple(s.key for s in s_bf),
                lambda c, bk, ht=ht: vop(lambda eng, ht=ht, c=c, bk=bk: eng.tensor_tensor(
                    x_cm[ht].ap(0, 128, 1 + c * CK, 1 + (c + 1) * CK),
                    x_cm[ht].ap(0, 128, 1 + c * CK, 1 + (c + 1) * CK),
                    bank(bk), OP.add),
                    reads=(f"ps{bk}", f"x{ht}"), writes=(f"x{ht}",)))
        outer_ln(None)
        AL.release()
        P.epoch()

    def proc_qk(l, qr, sqb, rota, rotb, cs, ss, spp, dst, par):
        pr0 = 32 * par
        rkey = f"rowT{par}"
        sbk = 7 if par == 0 else 4
        for c in range(NC4):
            vop(lambda eng, c=c: eng.activation(
                sqb.ap(), qr.ap(0, 128, c * CK, (c + 1) * CK), AF.Square),
                reads=(qr.key,), writes=(sqb.key,), eng="act")
            mm1(bank(sbk, 0, 2, 0, 512), e2.ap(0, 128, 0, 2),
                sqb.ap(), (sqb.key, "e2"), f"ps{sbk}")
            vop(lambda eng, c=c: eng.tensor_scalar(
                rowT.ap(pr0, pr0 + 2, c * CK, (c + 1) * CK), bank(sbk, 0, 2, 0, 512),
                1e-24, None, OP.max),
                reads=(f"ps{sbk}",), writes=(rkey,))
        vop(lambda eng: eng.activation(rowT.ap(pr0, pr0 + 2), rowT.ap(pr0, pr0 + 2),
                                       AF.Sqrt),
            reads=(rkey,), writes=(rkey,), eng="act")
        vop(lambda eng: eng.reciprocal(rowT.ap(pr0, pr0 + 2), rowT.ap(pr0, pr0 + 2)),
            reads=(rkey,), writes=(rkey,))
        rbk = 6 if par == 0 else 5
        pbk = 5 if par == 0 else 6
        for c in range(NC4):
            mm1(bank(rbk), eb.ap(pr0, pr0 + 2, 0, 128),
                rowT.ap(pr0, pr0 + 2, c * CK, (c + 1) * CK),
                (rkey, "eb"), f"ps{rbk}")
            vop(lambda eng, c=c: eng.tensor_tensor(
                qr.ap(0, 128, c * CK, (c + 1) * CK),
                qr.ap(0, 128, c * CK, (c + 1) * CK), bank(rbk), OP.mult),
                reads=(qr.key, f"ps{rbk}"), writes=(qr.key,))
            vop(lambda eng, c=c: eng.scalar_tensor_tensor(
                rota.ap(), qr.ap(0, 128, c * CK, (c + 1) * CK), spp.ap(0, 128, 0, 1),
                cs.ap(0, 128, c * CK, (c + 1) * CK), OP.mult, OP.mult),
                reads=(qr.key, spp.key, cs.key), writes=(rota.key,))
            vop(lambda eng, c=c: eng.scalar_tensor_tensor(
                rotb.ap(), qr.ap(0, 128, c * CK, (c + 1) * CK), spp.ap(0, 128, 0, 1),
                ss.ap(0, 128, c * CK, (c + 1) * CK), OP.mult, OP.mult),
                reads=(qr.key, spp.key, ss.key), writes=(rotb.key,))
            mm1(bank(pbk), prot.ap(), rotb.ap(), (rotb.key, "prot"), f"ps{pbk}")
            vop(lambda eng, c=c: eng.tensor_tensor(
                dst.ap(0, 128, c * CK, (c + 1) * CK),
                rota.ap(), bank(pbk), OP.add),
                reads=(rota.key, f"ps{pbk}"), writes=(dst.key,))

    def attention(l, q_bf, k_bf, v_tm, o_bf, sim_f, exf, at_tp, att_t, smax, ssum):
        PAIRS = [(4, 5), (6, 7), (0, 1), (2, 3)]
        units = []
        for w in range(4):
            for ri in range(4):
                qt = w * 4 + ri
                for h in range(8):
                    units.append((w, ri, qt, h))
        avg = {}

        def geo(i):
            w, ri, qt, h = units[i]
            n = 5 if w > 0 else ri + 1
            kb0 = (w - 1) * 4 + ri if w > 0 else 0
            return w, ri, qt, h, n, kb0

        def emit_sim(i):
            w, ri, qt, h, n, kb0 = geo(i)
            th, par = h // 2, h % 2
            p0 = 64 * par
            sA, sB = PAIRS[i % 4]
            q0 = qt * 128
            W = n * 128
            W1 = min(W, 512)
            lt = q_bf[th].ap(p0, p0 + 64, q0, q0 + 128)
            mm1(bank(sA, 0, 128, 0, W1), lt,
                k_bf[th].ap(p0, p0 + 64, kb0 * 128, kb0 * 128 + W1),
                (q_bf[th].key, k_bf[th].key), f"ps{sA}")
            if W > 512:
                mm1(bank(sB, 0, 128, 0, W - 512), lt,
                    k_bf[th].ap(p0, p0 + 64, kb0 * 128 + 512, kb0 * 128 + W),
                    (q_bf[th].key, k_bf[th].key), f"ps{sB}")

        def emit_soft(i):
            w, ri, qt, h, n, kb0 = geo(i)
            sA, sB = PAIRS[i % 4]
            W = n * 128
            sf = sim_f[i % 4]
            ex = exf[i % 4]
            atp = at_tp[i % 4]

            def pch(c):
                if c < 4:
                    return bank(sA, 0, 128, c * 128, (c + 1) * 128), f"ps{sA}"
                return bank(sB, 0, 128, (c - 4) * 128, (c - 3) * 128), f"ps{sB}"
            lastc = n - 1
            if w > 0:
                pap, pkey = pch(0)
                vop(lambda eng, pap=pap, sf=sf: eng.tensor_tensor(
                    sf.ap(0, 128, 0, 128), pap,
                    MASKS[0].ap(0, 128, 0, 128), OP.add),
                    reads=(pkey, "masks"), writes=(sf.key,))
            pap, pkey = pch(lastc)
            vop(lambda eng, pap=pap, sf=sf, lastc=lastc: eng.tensor_tensor(
                sf.ap(0, 128, lastc * 128, (lastc + 1) * 128), pap,
                MASKS[0].ap(0, 128, 128, 256), OP.add),
                reads=(pkey, "masks"), writes=(sf.key,))
            c0m = 1 if w > 0 else 0
            if lastc > c0m:
                nm_ = min(4, lastc) - c0m
                if nm_ > 0:
                    vop(lambda eng, c0m=c0m, nm_=nm_, sA=sA, sf=sf: eng.copy(
                        sf.ap(0, 128, c0m * 128, (c0m + nm_) * 128),
                        bank(sA, 0, 128, c0m * 128, (c0m + nm_) * 128)),
                        reads=(f"ps{sA}",), writes=(sf.key,), eng="act")
                if lastc > 4:
                    vop(lambda eng, sB=sB, sf=sf: eng.copy(
                        sf.ap(0, 128, 512, 640), bank(sB, 0, 128, 0, 128)),
                        reads=(f"ps{sB}",), writes=(sf.key,), eng="act")
            vop(lambda eng, W=W, h=h, sf=sf: eng.reduce_max(
                smax.ap(0, 128, h, h + 1), sf.ap(0, 128, 0, W), axis=AX,
                negate=True),
                reads=(sf.key,), writes=("smax",))
            vop(lambda eng, W=W, h=h, sf=sf, ex=ex: eng.activation(
                ex.ap(0, 128, 0, W), sf.ap(0, 128, 0, W), AF.Exp,
                bias=smax.ap(0, 128, h, h + 1),
                accum_out=ssum.ap(0, 128, h, h + 1)),
                reads=(sf.key, "smax"), writes=(ex.key, "ssum"), eng="act")
            vop(lambda eng, h=h: eng.reciprocal(
                smax.ap(0, 128, 8 + h, 9 + h), ssum.ap(0, 128, h, h + 1)),
                reads=("ssum",), writes=("smaxr",))
            vop(lambda eng, W=W, h=h, atp=atp, ex=ex: eng.activation(
                atp.ap(0, 128, 0, W), ex.ap(0, 128, 0, W), AF.Copy,
                scale=smax.ap(0, 128, 8 + h, 9 + h)),
                reads=(ex.key, "smaxr"), writes=(atp.key,), eng="act")

        def emit_tail(i):
            w, ri, qt, h, n, kb0 = geo(i)
            th, par = h // 2, h % 2
            p0 = 64 * par
            sA, sB = PAIRS[i % 4]
            q0 = qt * 128
            W = n * 128
            W1 = min(W, 512)
            atp = at_tp[i % 4]
            att = att_t[i % 4]
            for c in range(n):
                bk = sA if c < 4 else sB
                cc = (c % 4) * 128
                mm1(bank(bk, 0, 128, cc, cc + 128),
                    atp.ap(0, 128, c * 128, (c + 1) * 128), ident_b.ap(),
                    (atp.key, "identb"), f"ps{bk}")
            vop(lambda eng, W1=W1, att=att, sA=sA: eng.tensor_copy(
                att.ap(0, 128, 0, W1), bank(sA, 0, 128, 0, W1)),
                reads=(f"ps{sA}",), writes=(att.key,))
            if n > 4:
                vop(lambda eng, att=att, sB=sB: eng.copy(
                    att.ap(0, 128, 512, 640), bank(sB, 0, 128, 0, 128)),
                    reads=(f"ps{sB}",), writes=(att.key,), eng="act")
            pe = (i - par) % 4
            avb = PAIRS[pe][1]
            g = Group(f"ps{avb}", inc_each=True)
            for c in range(n):
                g.member(bank(avb, p0, p0 + 64, 384, 512),
                         v_tm[th].ap(0, 128, (kb0 + c) * 128 + p0,
                                     (kb0 + c) * 128 + p0 + 64),
                         att.ap(0, 128, c * 128, (c + 1) * 128),
                         (v_tm[th].key, att.key),
                         last=(c == n - 1))
            if par == 1:
                vop(lambda eng, th=th, q0=q0, avb=avb: eng.tensor_copy(
                    o_bf[th].ap(0, 128, q0, q0 + 128),
                    bank(avb, 0, 128, 384, 512)),
                    reads=(f"ps{avb}",), writes=(o_bf[th].key,))

        NU = len(units)
        for i in range(NU + 2):
            if i < NU:
                emit_sim(i)
            if 1 <= i < NU + 1:
                emit_soft(i - 1)
            if i >= 2:
                emit_tail(i - 2)

    def ffn_layer(b, l):
        AL.mark()
        q_bf = [AL.buf(BF16, S, f"qbf{t}") for t in range(4)]
        k_bf = [AL.buf(BF16, S, f"kbf{t}") for t in range(4)]
        v_tm = [AL.buf(BF16, S, f"vtm{t}") for t in range(4)]

        AL.mark()
        TABS[0] = {nm: AL.buf(BF16, S, "tab" + nm)
                   for nm in ("csq", "ssq", "csk", "ssk")}
        tabs = TABS[0]
        AL.mark()
        thi = AL.buf(F32, S, "thi")
        tlo = AL.buf(F32, S, "tlo")
        for nm in ("csq", "ssq", "csk", "ssk"):
            dma("syncq", thi.ap(), tabs_in[nm + "_hi"][:, :], (), ("thi",))
            dma("syncq", tlo.ap(), tabs_in[nm + "_lo"][:, :], (), ("tlo",))
            vop(lambda eng, nm=nm: eng.tensor_tensor(tabs[nm].ap(), thi.ap(),
                                                     tlo.ap(), OP.add),
                reads=("thi", "tlo"), writes=(tabs[nm].key,))
        AL.release()
        h_bf = [AL.buf(BF16, S, f"hb{t}") for t in range(NT)]
        wqs = [AL.buf(BF16, CH, "wqs0")]
        qraw = [AL.buf(F32, S, "qraw0"), AL.buf(F32, S, "qraw1")]
        vcm = AL.buf(BF16, CK, "vcm")
        _sq = AL.buf(BF16, CK, "sqql")
        sqb = [_sq, _sq]
        rotb = sqb
        rota = [AL.buf(F32, CK, "rota0"), AL.buf(F32, CK, "rota1")]
        for t in range(NT):
            so = 1 if t < 4 else 0
            vop(lambda eng, t=t, so=so: eng.copy(h_bf[t].ap(),
                                                 x_cm[t].ap(0, 128, so, so + S)),
                reads=(f"x{t}",), writes=(h_bf[t].key,), eng="act")
        wq = wqkv[l].rearrange("(k p) n -> p k n", p=128)
        pending = [None]

        def flush_pending():
            if pending[0] is not None:
                m0 = pending[0]
                if m0 < 4:
                    proc_qk(l, qraw[m0 % 2], sqb[m0 % 2], rota[m0 % 2], rotb[m0 % 2],
                            tabs["csq"], tabs["ssq"], qs_pp[l], q_bf[m0], m0 % 2)
                else:
                    proc_qk(l, qraw[m0 % 2], sqb[m0 % 2], rota[m0 % 2], rotb[m0 % 2],
                            tabs["csk"], tabs["ssk"], ks_pp[l], k_bf[m0 - 4], m0 % 2)
                pending[0] = None

        for m in range(12):
            ws = wqs[0]
            dma("poolq", ws.ap().rearrange("p (k n) -> p k n", n=128),
                wq[:, :, m * 128:(m + 1) * 128], (), (ws.key,))

            def qkv_evac(c, bk, m=m):
                if m < 8:
                    qr = qraw[m % 2]
                    vop(lambda eng, c=c, bk=bk, qr=qr: eng.copy(
                        qr.ap(0, 128, c * CK, (c + 1) * CK), bank(bk)),
                        reads=(f"ps{bk}",), writes=(qr.key,), eng="act")
                else:
                    t = m - 8
                    vop(lambda eng, c=c, bk=bk: eng.copy(vcm.ap(), bank(bk)),
                        reads=(f"ps{bk}",), writes=("vcm",), eng="act")
                    for blk in range(4):
                        bk2 = 4 + (blk % 2)
                        mm1(bank(bk2, 0, 128, 0, 128),
                            vcm.ap(0, 128, blk * 128, (blk + 1) * 128),
                            ident_b.ap(), ("vcm", "identb"), f"ps{bk2}")
                        vop(lambda eng, t=t, c=c, blk=blk, bk2=bk2: eng.tensor_copy(
                            v_tm[t].ap(0, 128, (c * 4 + blk) * 128,
                                       (c * 4 + blk + 1) * 128),
                            bank(bk2, 0, 128, 0, 128)),
                            reads=(f"ps{bk2}",), writes=(v_tm[t].key,))
            mm_kmajor(
                8,
                lambda k, ws=ws: ws.ap(0, 128, k * 128, (k + 1) * 128),
                lambda k, c: h_bf[k].ap(0, 128, c * CK, (c + 1) * CK),
                [0, 1, 2, 3],
                lambda k, ws=ws: (ws.key,) + tuple(h.key for h in h_bf),
                qkv_evac)
            flush_pending()
            if m < 8:
                pending[0] = m
        flush_pending()
        AL.release()
        P.epoch()

        AL.mark()
        MASKS[0] = AL.buf(BF16, 2 * 128, "masks")
        for i in range(2):
            dma("poolq", MASKS[0].ap(0, 128, i * 128, (i + 1) * 128),
                masks_in[i, :, :], (), ("masks",))
        o_bf = [AL.buf(BF16, S, f"obf{t}") for t in range(4)]
        sim_f = [AL.buf(F32, 640, f"simf{i}") for i in range(4)]
        exf = [AL.buf(F32, 640, f"exf{i}") for i in range(4)]
        at_tp = [AL.buf(BF16, 640, f"attp{i}") for i in range(4)]
        att_t = [AL.buf(BF16, 640, f"att_t{i}") for i in range(4)]
        smax = AL.buf(F32, 16, "smax")
        ssum = AL.buf(F32, 16, "ssum")
        attention(l, q_bf, k_bf, v_tm, o_bf, sim_f, exf, at_tp, att_t, smax, ssum)

        wqs2 = [AL.buf(BF16, 512, f"wos{i}") for i in range(2)]
        wo_r = wof[l].rearrange("(k p) n -> p k n", p=128)
        for m in range(NT):
            ws = wqs2[m % 2]
            dma("poolq", ws.ap().rearrange("p (k n) -> p k n", n=128),
                wo_r[:, :, m * 128:(m + 1) * 128], (), (ws.key,))
            mm_kmajor(
                4,
                lambda k, ws=ws: ws.ap(0, 128, k * 128, (k + 1) * 128),
                lambda k, c: o_bf[k].ap(0, 128, c * CK, (c + 1) * CK),
                [0, 1, 2, 3],
                lambda k, ws=ws: (ws.key,) + tuple(o.key for o in o_bf),
                lambda c, bk, m=m: vop(lambda eng, m=m, c=c, bk=bk: eng.tensor_tensor(
                    x_cm[m].ap(0, 128, 1 + c * CK, 1 + (c + 1) * CK),
                    x_cm[m].ap(0, 128, 1 + c * CK, 1 + (c + 1) * CK),
                    bank(bk), OP.add),
                    reads=(f"ps{bk}", f"x{m}"), writes=(f"x{m}",)))
        AL.release()
        AL.release()
        P.epoch()

        AL.mark()
        hb = [AL.buf(BF16, S, f"h2b{t}") for t in range(NT)]
        CST[0] = [AL.buf(BF16, CK, "cst0"), AL.buf(BF16, CK, "cst1")]
        fga = AL.buf(F32, CK, "lnfga")
        for t in range(NT):
            so = 1 if t < 4 else 0
            vop(lambda eng, t=t, so=so: eng.copy(hb[t].ap(),
                                                 x_cm[t].ap(0, 128, so, so + S)),
                reads=(f"x{t}",), writes=(hb[t].key,), eng="act")
        for c in range(NC4):
            stats_chunk(c, hb,
                        lambda t, c=c: x_cm[t].ap(0, 128, (1 if t < 4 else 0) + c * CK,
                                                  (1 if t < 4 else 0) + (c + 1) * CK))
            bcast_rows()
            for t in range(NT):
                so = 1 if t < 4 else 0
                vop(lambda eng, t=t, c=c, so=so: eng.tensor_tensor(
                    fga.ap(), x_cm[t].ap(0, 128, so + c * CK, so + (c + 1) * CK),
                    bank(5), OP.mult),
                    reads=(f"x{t}", "ps5"), writes=("lnfga",))
                vop(lambda eng: eng.tensor_tensor(
                    fga.ap(), fga.ap(), bank(6), OP.subtract),
                    reads=("lnfga", "ps6"), writes=("lnfga",))
                vop(lambda eng, t=t, c=c, l=l: eng.tensor_scalar(
                    hb[t].ap(0, 128, c * CK, (c + 1) * CK), fga.ap(),
                    g_pp[l].ap(0, 128, t, t + 1), bb_pp[l].ap(0, 128, t, t + 1),
                    OP.mult, OP.add),
                    reads=("lnfga", g_pp[l].key, bb_pp[l].key), writes=(hb[t].key,))
        h3 = [AL.buf(BF16, 1024, f"h3_{j}") for j in range(NJ)]
        w1s = [AL.buf(BF16, 2 * CH, f"w1s{i}") for i in range(3)]
        w2sl = [AL.buf(BF16, 512, f"w2sl{i}") for i in range(4)]
        gel = [AL.buf(BF16, CK, f"gel{i}") for i in range(2)]
        w1r = w1[l].rearrange("(k p) n -> p k n", p=128)
        for cc in range(2):
            c0 = cc * 1024
            vop(lambda eng: eng.memset(h3[21].ap(), 0.0), writes=(h3[21].key,))
            for j in range(NJ):
                ws1 = w1s[j % 3]
                M = 128 if j < 21 else 42
                dma("poolq",
                    ws1.ap(0, 128, 0, 1024).rearrange("p (k n) -> p k n", n=128)[:, :, 0:M],
                    w1r[:, :, j * 128:j * 128 + M], (), (ws1.key,))
                dma("poolq",
                    ws1.ap(0, 128, 1024, 2048).rearrange("p (k n) -> p k n", n=128)[:, :, 0:M],
                    w1r[:, :, HID + j * 128:HID + j * 128 + M], (), (ws1.key,))
                ba = [0, 1] if j % 2 == 0 else [4, 5]
                bg = [2, 3] if j % 2 == 0 else [6, 7]
                mm_kmajor(
                    8,
                    lambda k, ws1=ws1, M=M: ws1.ap(0, 128, k * 128, k * 128 + M),
                    lambda k, sub, c0=c0: hb[k].ap(0, 128, c0 + sub * CK,
                                                   c0 + (sub + 1) * CK),
                    ba,
                    lambda k, ws1=ws1: (ws1.key,) + tuple(h.key for h in hb),
                    lambda sub, bk: None, pslice=(0, 128 if j < 21 else 42))
                mm_kmajor(
                    8,
                    lambda k, ws1=ws1, M=M: ws1.ap(0, 128, 1024 + k * 128,
                                                   1024 + k * 128 + M),
                    lambda k, sub, c0=c0: hb[k].ap(0, 128, c0 + sub * CK,
                                                   c0 + (sub + 1) * CK),
                    bg,
                    lambda k, ws1=ws1: (ws1.key,) + tuple(h.key for h in hb),
                    lambda sub, bk: None, pslice=(0, 128 if j < 21 else 42))
                for sub in range(2):
                    gl = gel[sub]
                    vop(lambda eng, M=M, bkg=bg[sub], j=j, l=l, gl=gl: eng.activation(
                        gl.ap(0, M, 0, CK), bank(bg[sub] if False else bkg, 0, M),
                        AF.Gelu, bias=b1g_pp[l].ap(0, M, j, j + 1)),
                        reads=(f"ps{bg[sub]}", b1g_pp[l].key), writes=(gl.key,),
                        eng="act")
                    vop(lambda eng, M=M, bka=ba[sub], j=j, l=l, sub=sub, gl=gl:
                        eng.scalar_tensor_tensor(
                            h3[j].ap(0, M, sub * CK, (sub + 1) * CK),
                            bank(bka, 0, M), b1a_pp[l].ap(0, M, j, j + 1),
                            gl.ap(0, M, 0, CK), OP.add, OP.mult),
                        reads=(f"ps{ba[sub]}", b1a_pp[l].key, gl.key),
                        writes=(h3[j].key,))
            for half in range(2):
                w2g = [Group(f"ps{bi}", inc_each=True) for bi in range(8)]
                for j in range(NJ):
                    ws2 = w2sl[j % 4]
                    Mk = 128 if j < 21 else 42
                    if j == 21:
                        vop(lambda eng, ws2=ws2: eng.memset(ws2.ap(), 0.0),
                            writes=(ws2.key,))
                    dma("poolq", ws2.ap(0, Mk, 0, 512),
                        w2[l, j * 128:j * 128 + Mk, half * 512:(half + 1) * 512],
                        (), (ws2.key,))
                    for hti in range(4):
                        for sub in range(2):
                            bi = hti * 2 + sub
                            w2g[bi].member(
                                bank(bi),
                                ws2.ap(0, 128, hti * 128, (hti + 1) * 128),
                                h3[j].ap(0, 128, sub * CK, (sub + 1) * CK),
                                (ws2.key, h3[j].key), last=(j == NJ - 1))
                for hti in range(4):
                    ht = half * 4 + hti
                    for sub in range(2):
                        bi = hti * 2 + sub
                        vop(lambda eng, ht=ht, c0=c0, sub=sub, bi=bi, l=l:
                            eng.scalar_tensor_tensor(
                                x_cm[ht].ap(0, 128, 1 + c0 + sub * CK,
                                            1 + c0 + (sub + 1) * CK), bank(bi),
                                b2_pp[l].ap(0, 128, ht, ht + 1),
                                x_cm[ht].ap(0, 128, 1 + c0 + sub * CK,
                                            1 + c0 + (sub + 1) * CK),
                                OP.add, OP.add),
                            reads=(f"ps{bi}", b2_pp[l].key, f"x{ht}"),
                            writes=(f"x{ht}",))
        AL.release()
        P.epoch()

    def final_ln_out(b):
        AL.mark()
        ld = [AL.buf(F32, S, f"rb{i}") for i in range(2)]
        for t in range(NT):
            L = ld[t % 2]
            dma("syncq", L.ap(), x1sp[t, :, :], ("x1sp",), (L.key,))
            vop(lambda eng, t=t, L=L: eng.tensor_tensor(
                x_cm[t].ap(0, 128, 1, S + 1), x_cm[t].ap(0, 128, 1, S + 1),
                L.ap(), OP.add),
                reads=(f"x{t}", L.key), writes=(f"x{t}",))
        AL.release()
        AL.mark()
        CST[0] = [AL.buf(BF16, CK, "cst0"), AL.buf(BF16, CK, "cst1")]
        for c in range(NC4):
            stats_chunk(c, None,
                        lambda t, c=c: x_cm[t].ap(0, 128, 1 + c * CK, 1 + (c + 1) * CK))
            bcast_rows()
            for t in range(NT):
                vop(lambda eng, t=t, c=c: eng.tensor_tensor(
                    x_cm[t].ap(0, 128, 1 + c * CK, 1 + (c + 1) * CK),
                    x_cm[t].ap(0, 128, 1 + c * CK, 1 + (c + 1) * CK),
                    bank(5), OP.mult),
                    reads=(f"x{t}", "ps5"), writes=(f"x{t}",))
                vop(lambda eng, t=t, c=c: eng.tensor_tensor(
                    x_cm[t].ap(0, 128, 1 + c * CK, 1 + (c + 1) * CK),
                    x_cm[t].ap(0, 128, 1 + c * CK, 1 + (c + 1) * CK),
                    bank(6), OP.subtract),
                    reads=(f"x{t}", "ps6"), writes=(f"x{t}",))
        for t in range(NT):
            dma("syncq", y_out[b, t * 128:(t + 1) * 128, :],
                x_cm[t].ap(0, 128, 1, S + 1), (f"x{t}",), (f"y{b}_{t}",))
        AL.release()
        P.epoch()

    for b in range(NB):
        load_x(b)
        ssm(b)
        for t in range(NT):
            dma("syncq", x1sp[t, :, :], x_cm[t].ap(0, 128, 1, S + 1),
                (f"x{t}",), ("x1sp",))
        for l in range(2):
            ffn_layer(b, l)
        final_ln_out(b)

    fin = {a: P.count[a] for a in ACCOUNTS if P.count[a]}
    P.ops["syncq"].append((fin, lambda eng: None, 0, "sq0"))

    with nc.Block() as block:
        @block.tensor
        def _(eng):
            P.replay("te", eng, sems)

        @block.vector
        def _(eng):
            P.replay("dve", eng, sems)

        @block.scalar
        def _(eng):
            P.replay("act", eng, sems)

        @block.sync
        def _(eng):
            P.replay("syncq", eng, sems)

        @block.gpsimd
        def _(eng):
            P.replay("poolq", eng, sems)

    ctx.close()
    return nc


_NC_CACHE = [None]


def kernel(L0_attn_x, L0_attn_Wf, L0_attn_Wi, L0_attn_Wv, L0_attn_Wo,
           L0_ffn_wqkv, L0_ffn_q_scale, L0_ffn_k_scale, L0_ffn_wo,
           L0_ffn_ln_g, L0_ffn_ln_b, L0_ffn_w1, L0_ffn_b1, L0_ffn_w2,
           L0_ffn_b2):
    x = np.ascontiguousarray(np.asarray(L0_attn_x, dtype=np.float32))
    cast = lambda a: np.ascontiguousarray(np.asarray(a, dtype=np.float32))
    tables = _mk_tables()
    e2h = np.zeros((128, 2), np.float32)
    e2h[0:64, 0] = 1.0
    e2h[64:128, 1] = 1.0
    ebh = np.zeros((2, 128), np.float32)
    ebh[0, 0:64] = 1.0
    ebh[1, 64:128] = 1.0
    castT = lambda a: np.ascontiguousarray(np.asarray(a, dtype=np.float32).T)
    common = dict(
        Wf=castT(L0_attn_Wf), Wi=castT(L0_attn_Wi), Wv=castT(L0_attn_Wv),
        Wo=castT(L0_attn_Wo), wqkv=cast(L0_ffn_wqkv), wof=cast(L0_ffn_wo),
        w1=cast(L0_ffn_w1), w2=cast(L0_ffn_w2), qsc=cast(L0_ffn_q_scale),
        ksc=cast(L0_ffn_k_scale), lng=cast(L0_ffn_ln_g), lnb=cast(L0_ffn_ln_b),
        b1=cast(L0_ffn_b1), b2=cast(L0_ffn_b2),
        masks=_mk_masks(),
        idf=np.eye(128, dtype=np.float32),
        idb=np.eye(128, dtype=np.float32).astype(BFD),
        prot=_mk_prot().astype(BFD),
        onec=np.ones((128, 1), np.float32).astype(BFD),
        oner=np.ones((1, 128), np.float32),
        e2=e2h.astype(BFD),
        eb=ebh,
        **tables,
    )
    if _NC_CACHE[0] is None:
        _NC_CACHE[0] = build_program()
    nc = _NC_CACHE[0]
    in_maps = []
    for core in range(8):
        m = dict(common)
        m["x"] = np.ascontiguousarray(x[2 * core:2 * core + 2].transpose(0, 2, 1))
        in_maps.append(m)
    res = run_bass_kernel_spmd(nc, in_maps, core_ids=list(range(8)))
    return np.ascontiguousarray(
        np.concatenate([r["y"] for r in res.results], axis=0).transpose(0, 2, 1))
